# version 22
# baseline (speedup 1.0000x reference)
import numpy as np
from contextlib import ExitStack

import concourse.bass as bass
import concourse.mybir as mybir
from concourse.bass_utils import run_bass_kernel_spmd

F32 = mybir.dt.float32
BF16 = mybir.dt.bfloat16
I32 = mybir.dt.int32
AF = mybir.ActivationFunctionType
ALU = mybir.AluOpType

B, L, D = 4, 2048, 4096
DH = D // 2
NST = 16
RK = 128
NDT = DH // 128
TQ = 512
NQ = L // TQ
EPS = 1e-5
TS = 512
NTS = L // TS
NCH = D // 128
NOT = DH // 128

DRY = [False]


class _Dummy:
    def __getattr__(self, name):
        return lambda *a, **kw: self

    def __call__(self, *a, **kw):
        return self


class Sem:
    def __init__(self, es, nc, name):
        self.h = es.enter_context(nc.semaphore(name))
        self.n = 0

    def inc(self, inst, k=1):
        self.n += k
        if not DRY[0]:
            inst.then_inc(self.h, k)
        return self.n


def W(eng, sem, val_or_dict, key=None):
    if DRY[0]:
        return
    v = val_or_dict if key is None else val_or_dict[key]
    eng.wait_ge(sem.h, v)


def build_nc():
    nc = bass.Bass("TRN2", target_bir_lowering=False, debug=False)

    def din(name, shape, dt=F32):
        return nc.dram_tensor(name, shape, dt, kind="ExternalInput").ap()

    xTd = din("xTd", [DH, L])
    xpwT = din("xpwT", [DH, 160])
    dtwT = din("dtwT", [RK, DH])
    Aarr_i = din("Aarr", [128, NDT * NST])
    Darr_i = din("Darr", [128, NDT])
    dtbarr_i = din("dtbarr", [128, NDT])
    WTh = din("WTh", [D, DH])
    biasFh = din("biasFh", [128, NOT])
    lATf_i = din("lATf", [D, 48])
    lBTh_i = din("lBTh", [16, 3 * DH])
    lsirep_i = din("lsirep", [4, 4], I32)
    biota_i = din("biota", [4, 4])
    kiota_i = din("kiota", [4, 3])
    bsel16_i = din("bsel16", [4, 16])
    selBC_i = din("selBC", [32, 32 * 128])
    onesrow_i = din("onesrow", [1, 128])
    onescol_b_i = din("onescol_b", [128, 1], BF16)

    oq = nc.dram_tensor("oq", [DH, L], F32, kind="ExternalOutput").ap()

    cc_sxq = [nc.dram_tensor(f"cc_sxq{q}", [162, TQ], F32).ap()
              for q in range(NQ)]
    cc_sxoq = [nc.dram_tensor(f"cc_sxoq{q}", [162, TQ], F32).ap()
               for q in range(NQ)]
    cc_x1q = [nc.dram_tensor(f"cc_x1q{q}", [DH + 2, TQ], F32).ap()
              for q in range(NQ)]
    cc_x1oq = [nc.dram_tensor(f"cc_x1oq{q}", [2 * (DH + 2), TQ], F32).ap()
               for q in range(NQ)]
    wbf = nc.dram_tensor("wbf", [D, DH], BF16).ap()

    es = ExitStack()
    with es:
        def sb(name, shape, dt=F32):
            return es.enter_context(nc.sbuf_tensor(name, shape, dt))

        onesrowt = sb("onesrowt", [1, 128])
        onescolbt = sb("onescolbt", [128, 1], BF16)
        onescolf = sb("onescolf", [128, 1])
        epscol = sb("epscol", [128, 1])
        stq = es.enter_context(nc.sbuf_tensor("stq", [2, TQ], F32))
        stq2 = es.enter_context(nc.sbuf_tensor("stq2", [2, TQ], F32))
        x1pre = es.enter_context(nc.sbuf_tensor("x1pre", [128, 2, TQ], F32))
        biasFt = sb("biasFt", [128, NOT])
        lsit = sb("lsit", [4, 4], I32)
        lsif = sb("lsif", [4, 4])
        biotat = sb("biotat", [4, 4])
        kiotat = sb("kiotat", [4, 3])
        bsel16t = sb("bsel16t", [4, 16])
        es12 = ExitStack()

        def sb12(name, shape, dt=F32):
            return es12.enter_context(nc.sbuf_tensor(name, shape, dt,
                                                     side="right"))

        dtwTb = sb12("dtwTb", [128, NDT, 128], BF16)
        An = sb12("An", [128, NDT * NST])
        Darr = sb12("Darr_s", [128, NDT])
        dtbarr = sb12("dtbarr_s", [128, NDT])
        selBCt = sb12("selBCt", [32, 32 * 128])
        mrep = sb12("mrep", [128, L])
        rrep = sb12("rrep", [128, L])
        xp0b = sb12("xp0b", [128, L], BF16)
        xpBCf = sb12("xpBCf", [32, L])

        sLD = Sem(es, nc, "sLD")
        sGD = Sem(es, nc, "sGD")
        sCC = Sem(es, nc, "sCC")
        sT = Sem(es, nc, "sT")
        sS = Sem(es, nc, "sS")
        sV = Sem(es, nc, "sV")
        sX = [Sem(es, nc, "sX0"), Sem(es, nc, "sX1")]
        sY = [Sem(es, nc, "sY0"), Sem(es, nc, "sY1")]
        sXC = [Sem(es, nc, f"sXC{i}") for i in range(6)]
        sWs = [Sem(es, nc, "sW0"), Sem(es, nc, "sW1")]
        sR = [Sem(es, nc, "sR0"), Sem(es, nc, "sR1")]
        sLA = [Sem(es, nc, f"sLA{i}") for i in range(4)]
        sLB = Sem(es, nc, "sLB")
        sVV = Sem(es, nc, "sVV")
        sWL = [Sem(es, nc, "sWL0"), Sem(es, nc, "sWL1")]
        sWB = [Sem(es, nc, "sWB0"), Sem(es, nc, "sWB1")]
        sST = [Sem(es, nc, "sST0"), Sem(es, nc, "sST1")]
        ALLSEMS = ([sLD, sGD, sCC, sT, sS, sV] + sX + sY + sXC + sWs + sR
                   + sLA + [sLB, sVV] + sWL + sWB + sST)

        def run_block(bodies, blk=None):
            snap = [s.n for s in ALLSEMS]
            DRY[0] = True
            dummy = _Dummy()
            for _, fn in bodies:
                fn(dummy)
            DRY[0] = False
            finals = [s.n for s in ALLSEMS]
            for s, v in zip(ALLSEMS, snap):
                s.n = v
            for name, fn in bodies:
                getattr(blk, name)(fn)
            for s, f in zip(ALLSEMS, finals):
                assert s.n == f, f"sem plan mismatch: {s.n} != {f}"

        _es1 = ExitStack()
        with _es1:
            e1 = _es1.enter_context

            def sb1(name, shape, dt=F32):
                return e1(nc.sbuf_tensor(name, shape, dt, side="right"))

            xht = sb1("xht", [128, 2, TQ])
            xhb = sb1("xhb", [128, 2, TQ], BF16)
            xhsq = sb1("xhsq", [128, 2, TQ], BF16)
            xpwTf = sb1("xpwTf", [128, NDT, 160])
            dtwTf = sb1("dtwTf", [128, NDT, 128])
            xp0f = sb1("xp0f", [128, L])
            xpBCr = sb1("xpBCr", [32, L])
            sumS = sb1("sumS", [1, L])
            sqS = sb1("sqS", [1, L])
            murow = sb1("murow", [1, L])
            rstdrow = sb1("rstdrow", [1, L])
            tmprow = sb1("tmprow", [1, L])
            nxpws = sb1("nxpws", [128, 1])
            nxpwsBC = sb1("nxpwsBC", [32, 1])
            xpwTb = sb1("xpwTb", [128, NDT, 160], BF16)
            ps_sum = e1(nc.psum_tensor("ps_sum", [1, TQ], F32))
            ps_sq = e1(nc.psum_tensor("ps_sq", [1, TQ], F32))
            ps_xp0 = e1(nc.psum_tensor("ps_xp0", [128, TQ], F32))
            ps_xpBC = e1(nc.psum_tensor("ps_xpBC", [32, TQ], F32))
            ps_brd = e1(nc.psum_tensor("ps_brd", [128, 2, TQ], F32))
            blk1 = e1(nc.Block())
            M = {}
            ps_ws = ps_brd[:, 0, 0:1]
            ps_wsBC = ps_brd[0:32, 1, 0:1]

            def b1_sync(sync):
                for src, dst in [
                    (xpwT.rearrange("(a p) r -> p a r", p=128), xpwTf[:]),
                    (dtwT.rearrange("r (a d) -> r a d", d=128), dtwTf[:]),
                    (Aarr_i, An[:]),
                    (Darr_i, Darr[:]),
                    (dtbarr_i, dtbarr[:]),
                    (selBC_i, selBCt[:]),
                    (onesrow_i, onesrowt[:]),
                    (onescol_b_i, onescolbt[:]),
                ]:
                    sLD.inc(sync.dma_start(out=dst, in_=src), 16)
                M["ld_const"] = sLD.n
                for it in range(NQ * NDT):
                    q, dt_i = it // NDT, it % NDT
                    if it >= 2:
                        W(sync, sV, M, ("cast", it - 2))
                    sX[it % 2].inc(sync.dma_start(
                        out=xht[:, it % 2, :],
                        in_=xTd[dt_i * 128:(dt_i + 1) * 128,
                                q * TQ:(q + 1) * TQ]), 16)
                    M[("ldx", it)] = sX[it % 2].n

            def b1_vec(v):
                def vsyncb(inst):
                    val = sVV.inc(inst)
                    if not DRY[0]:
                        v.wait_ge(sVV.h, val)

                def casts(q):
                    for dt_i in range(NDT):
                        it = q * NDT + dt_i
                        W(v, sX[it % 2], M, ("ldx", it))
                        if it >= 2:
                            W(v, sT, M, ("tmm", it - 2))
                        inst = v.tensor_copy(out=xhb[:, it % 2, :],
                                             in_=xht[:, it % 2, :])
                        vsyncb(inst)
                        inst = v.tensor_tensor(out=xhsq[:, it % 2, :],
                                               in0=xhb[:, it % 2, :],
                                               in1=xhb[:, it % 2, :],
                                               op=ALU.mult)
                        sV.inc(inst)
                        M[("cast", it)] = sV.n

                def finq(q):
                    sl = slice(q * TQ, (q + 1) * TQ)
                    W(v, sGD, M, ("gback", q))
                    v.tensor_scalar(out=murow[:, sl], in0=sumS[:, sl],
                                    scalar1=1.0 / D, scalar2=None,
                                    op0=ALU.mult)
                    inst = v.tensor_scalar(out=tmprow[:, sl], in0=sqS[:, sl],
                                           scalar1=1.0 / D, scalar2=None,
                                           op0=ALU.mult)
                    vsyncb(inst)
                    inst = v.tensor_tensor(out=rstdrow[:, sl],
                                           in0=murow[:, sl],
                                           in1=murow[:, sl], op=ALU.mult)
                    vsyncb(inst)
                    inst = v.tensor_tensor(out=tmprow[:, sl],
                                           in0=tmprow[:, sl],
                                           in1=rstdrow[:, sl],
                                           op=ALU.subtract)
                    sV.inc(inst)
                    M[("vfin1", q)] = sV.n
                    W(v, sS, M, ("sfin1", q))
                    inst = v.tensor_scalar(out=tmprow[:, sl],
                                           in0=tmprow[:, sl], scalar1=-0.5,
                                           scalar2=None, op0=ALU.mult)
                    sV.inc(inst)
                    M[("vfin2", q)] = sV.n
                    if q == 0:
                        W(v, sS, M, "sws")
                        v.tensor_scalar(out=nxpws[:], in0=nxpws[:],
                                        scalar1=-1.0, scalar2=None,
                                        op0=ALU.mult)
                        v.tensor_scalar(out=nxpwsBC[:], in0=nxpwsBC[:],
                                        scalar1=-1.0, scalar2=None,
                                        op0=ALU.mult)
                        W(v, sS, M, "sexpA")
                        inst = v.tensor_scalar(out=An[:], in0=An[:],
                                               scalar1=-1.0, scalar2=None,
                                               op0=ALU.mult)
                        vsyncb(inst)
                    W(v, sS, M, ("brd", q))
                    inst = v.scalar_tensor_tensor(out=xp0f[:, sl],
                                                  in0=mrep[:, sl],
                                                  scalar=nxpws[:],
                                                  in1=xp0f[:, sl],
                                                  op0=ALU.mult, op1=ALU.add)
                    vsyncb(inst)
                    inst = v.tensor_tensor(out=xp0f[:, sl], in0=xp0f[:, sl],
                                           in1=rrep[:, sl], op=ALU.mult)
                    vsyncb(inst)
                    inst = v.scalar_tensor_tensor(out=xpBCf[:, sl],
                                                  in0=mrep[0:32, sl],
                                                  scalar=nxpwsBC[:],
                                                  in1=xpBCr[:, sl],
                                                  op0=ALU.mult, op1=ALU.add)
                    vsyncb(inst)
                    inst = v.tensor_tensor(out=xpBCf[:, sl],
                                           in0=xpBCf[:, sl],
                                           in1=rrep[0:32, sl], op=ALU.mult)
                    vsyncb(inst)
                    inst = v.tensor_copy(out=xp0b[:, sl], in_=xp0f[:, sl])
                    sV.inc(inst)
                    M[("vxp", q)] = sV.n

                v.memset(epscol[:], EPS)
                v.memset(onescolf[:], 1.0)
                W(v, sLD, M, "ld_const")
                for i in range(NDT):
                    v.tensor_copy(out=xpwTb[:, i, :], in_=xpwTf[:, i, :])
                    inst = v.tensor_copy(out=dtwTb[:, i, :],
                                         in_=dtwTf[:, i, :])
                sV.inc(inst)
                M["vconst"] = sV.n
                casts(0)
                casts(1)
                finq(0)
                casts(2)
                finq(1)
                casts(3)
                finq(2)
                finq(3)

            def b1_te(te):
                def stream(q):
                    for dt_i in range(NDT):
                        it = q * NDT + dt_i
                        W(te, sV, M, ("cast", it))
                        if q >= 1 and dt_i == 0:
                            W(te, sS, M, ("evac", q - 1))
                        st, sp = dt_i == 0, dt_i == NDT - 1
                        te.matmul(out=ps_sum[:], lhsT=onescolbt[:],
                                  rhs=xhb[:, it % 2, :], start=st, stop=sp)
                        te.matmul(out=ps_sq[:], lhsT=onescolbt[:],
                                  rhs=xhsq[:, it % 2, :], start=st, stop=sp)
                        te.matmul(out=ps_xp0[:],
                                  lhsT=xpwTb[:, dt_i, 0:128],
                                  rhs=xhb[:, it % 2, :], start=st, stop=sp)
                        inst = te.matmul(out=ps_xpBC[:],
                                         lhsT=xpwTb[:, dt_i, 128:160],
                                         rhs=xhb[:, it % 2, :],
                                         start=st, stop=sp)
                        sT.inc(inst)
                        M[("tmm", it)] = sT.n

                def tbrd(q):
                    sl = slice(q * TQ, (q + 1) * TQ)
                    if q == 0:
                        W(te, sS, M, "sws")
                    else:
                        W(te, sS, M, ("brd", q - 1))
                    W(te, sS, M, ("fin", q))
                    te.matmul(out=ps_brd[:, 0, :], lhsT=onesrowt[:],
                              rhs=murow[:, sl], start=True, stop=True)
                    inst = te.matmul(out=ps_brd[:, 1, :], lhsT=onesrowt[:],
                                     rhs=rstdrow[:, sl], start=True, stop=True)
                    sT.inc(inst)
                    M[("tbrd", q)] = sT.n

                W(te, sV, M, "vconst")
                for i in range(NDT):
                    te.matmul(out=ps_ws, lhsT=xpwTb[:, i, 0:128],
                              rhs=onescolbt[:], start=i == 0,
                              stop=i == NDT - 1)
                    inst = te.matmul(out=ps_wsBC, lhsT=xpwTb[:, i, 128:160],
                                     rhs=onescolbt[:], start=i == 0,
                                     stop=i == NDT - 1)
                sT.inc(inst)
                M["tws"] = sT.n
                stream(0)
                stream(1)
                tbrd(0)
                stream(2)
                tbrd(1)
                stream(3)
                tbrd(2)
                tbrd(3)

            def b1_sc(sc):
                def evac(q):
                    sl = slice(q * TQ, (q + 1) * TQ)
                    W(sc, sT, M, ("tmm", q * NDT + NDT - 1))
                    sc.activation(out=sumS[:, sl], in_=ps_sum[:], func=AF.Copy)
                    sc.activation(out=sqS[:, sl], in_=ps_sq[:], func=AF.Copy)
                    sc.activation(out=xp0f[:, sl], in_=ps_xp0[:],
                                  func=AF.Copy)
                    inst = sc.activation(out=xpBCr[:, sl], in_=ps_xpBC[:],
                                         func=AF.Copy)
                    sS.inc(inst)
                    M[("evac", q)] = sS.n

                def sfin(q):
                    sl = slice(q * TQ, (q + 1) * TQ)
                    W(sc, sV, M, ("vfin1", q))
                    inst = sc.activation(out=tmprow[:, sl], in_=tmprow[:, sl],
                                         func=AF.Ln, bias=epscol[0:1, :])
                    sS.inc(inst)
                    M[("sfin1", q)] = sS.n
                    W(sc, sV, M, ("vfin2", q))
                    inst = sc.activation(out=rstdrow[:, sl],
                                         in_=tmprow[:, sl], func=AF.Exp)
                    sS.inc(inst)
                    M[("fin", q)] = sS.n

                def brde(q):
                    sl = slice(q * TQ, (q + 1) * TQ)
                    W(sc, sT, M, ("tbrd", q))
                    sc.activation(out=mrep[:, sl], in_=ps_brd[:, 0, :],
                                  func=AF.Copy)
                    inst = sc.activation(out=rrep[:, sl],
                                         in_=ps_brd[:, 1, :], func=AF.Copy)
                    sS.inc(inst)
                    M[("brd", q)] = sS.n

                W(sc, sT, M, "tws")
                sc.activation(out=nxpws[:], in_=ps_ws, func=AF.Copy)
                inst = sc.activation(out=nxpwsBC[:], in_=ps_wsBC,
                                     func=AF.Copy)
                sS.inc(inst)
                M["sws"] = sS.n
                W(sc, sLD, M, "ld_const")
                inst = sc.activation(out=An[:], in_=An[:], func=AF.Exp)
                sS.inc(inst)
                M["sexpA"] = sS.n
                evac(0)
                sfin(0)
                evac(1)
                brde(0)
                sfin(1)
                evac(2)
                brde(1)
                sfin(2)
                evac(3)
                brde(2)
                sfin(3)
                brde(3)

            def b1_g(g):
                for q in range(NQ):
                    sl = slice(q * TQ, (q + 1) * TQ)
                    W(g, sS, M, ("evac", q))
                    sGD.inc(g.dma_start(out=cc_sxq[q][0:128, :],
                                        in_=xp0f[:, sl]), 16)
                    sGD.inc(g.dma_start(out=cc_sxq[q][128:160, :],
                                        in_=xpBCr[:, sl]), 16)
                    sGD.inc(g.dma_start(out=cc_sxq[q][160:161, :],
                                        in_=sumS[:, sl]), 16)
                    sGD.inc(g.dma_start(out=cc_sxq[q][161:162, :],
                                        in_=sqS[:, sl]), 16)
                    M[("gout", q)] = sGD.n
                    W(g, sGD, M, ("gout", q))
                    sCC.inc(g.collective_compute(
                        "AllReduce", ALU.add,
                        replica_groups=[[0, 1], [2, 3], [4, 5], [6, 7]],
                        ins=[cc_sxq[q].opt()], outs=[cc_sxoq[q].opt()]))
                    M[("cc1", q)] = sCC.n
                    W(g, sCC, M, ("cc1", q))
                    sGD.inc(g.dma_start(out=xp0f[:, sl],
                                        in_=cc_sxoq[q][0:128, :]), 16)
                    sGD.inc(g.dma_start(out=xpBCr[:, sl],
                                        in_=cc_sxoq[q][128:160, :]), 16)
                    sGD.inc(g.dma_start(out=sumS[:, sl],
                                        in_=cc_sxoq[q][160:161, :]), 16)
                    sGD.inc(g.dma_start(out=sqS[:, sl],
                                        in_=cc_sxoq[q][161:162, :]), 16)
                    M[("gback", q)] = sGD.n
                    W(g, sGD, M, ("gback", q))

            run_block([("sync", b1_sync), ("vector", b1_vec),
                       ("tensor", b1_te), ("scalar", b1_sc),
                       ("gpsimd", b1_g)], blk1)

        _es2 = ExitStack()
        with _es2:
            BrepA = _es2.enter_context(nc.sbuf_tensor("BrepA", [128, NST, TQ], F32, side="right"))
            CrepA = _es2.enter_context(nc.sbuf_tensor("CrepA", [128, NST, TQ], F32, side="right"))
            dtb2 = _es2.enter_context(nc.sbuf_tensor("dtb2", [128, 2, TQ], F32, side="right"))
            dtxb = _es2.enter_context(nc.sbuf_tensor("dtxb", [128, TQ], F32, side="right"))
            xnb = _es2.enter_context(nc.sbuf_tensor("xnb", [128, TQ], F32, side="right"))
            dab = _es2.enter_context(nc.sbuf_tensor("dab", [128, 2, TQ], F32, side="right"))
            dbxh = _es2.enter_context(nc.sbuf_tensor("dbxh", [128, 4, TQ], F32, side="right"))
            hall = _es2.enter_context(nc.sbuf_tensor("hall", [128, NST, TQ], F32, side="right"))
            hcb = _es2.enter_context(nc.sbuf_tensor("hcb", [128, TQ], F32, side="right"))
            yac = _es2.enter_context(nc.sbuf_tensor("yac", [128, TQ], F32, side="right"))
            hbound = _es2.enter_context(nc.sbuf_tensor("hbound", [128, NDT * NST], F32, side="right"))
            xt2 = _es2.enter_context(nc.sbuf_tensor("xt2", [128, 2, TQ], F32, side="right"))
            x1t = _es2.enter_context(nc.sbuf_tensor("x1t", [128, 2, TQ], F32, side="right"))
            x1sq = _es2.enter_context(nc.sbuf_tensor("x1sq", [128, 2, TQ], BF16, side="right"))
            x1sumS = _es2.enter_context(nc.sbuf_tensor("x1sumS", [1, TQ], F32, side="right"))
            x1sqS = _es2.enter_context(nc.sbuf_tensor("x1sqS", [1, TQ], F32, side="right"))
            wtf = _es2.enter_context(nc.sbuf_tensor("wtf", [128, 2, 512], F32, side="right"))
            wtb = _es2.enter_context(nc.sbuf_tensor("wtb", [128, 2, 512], BF16, side="right"))
            sfu = _es2.enter_context(nc.sbuf_tensor("sfu", [128, TQ], F32, side="right"))
            sfl = _es2.enter_context(nc.sbuf_tensor("sfl", [128, TQ], F32, side="right"))
            sfr = _es2.enter_context(nc.sbuf_tensor("sfr", [128, TQ], F32, side="right"))
            ps_dt = _es2.enter_context(nc.psum_tensor("ps_dt", [128, 2, TQ], F32))
            ps_bc = _es2.enter_context(nc.psum_tensor("ps_bc", [128, 2, TQ], F32))
            p_x1s = _es2.enter_context(nc.psum_tensor("p_x1s", [1, TQ], F32))
            p_x1q = _es2.enter_context(nc.psum_tensor("p_x1q", [1, TQ], F32))
            blk2 = _es2.enter_context(nc.Block())
            M2 = {}

            def b2_sync(sync):
                for q in range(NQ):
                    for dt_i in range(NDT):
                        it = q * NDT + dt_i
                        if it >= 2:
                            W(sync, sV, M2, ("x1", (it - 2) // NDT,
                                             (it - 2) % NDT))
                        sX[it % 2].inc(sync.dma_start(
                            out=xt2[:, it % 2, :],
                            in_=xTd[dt_i * 128:(dt_i + 1) * 128,
                                    q * TQ:(q + 1) * TQ]), 16)
                        M2[("ldx", q, dt_i)] = sX[it % 2].n
                        for wj in range(2):
                            wi = it * 2 + wj
                            if wi >= 2:
                                W(sync, sS, M2, ("wc", wi - 2))
                            rw, cw = wi // 4, wi % 4
                            sWL[wi % 2].inc(sync.dma_start(
                                out=wtf[:, wi % 2, :],
                                in_=WTh[rw * 128:(rw + 1) * 128,
                                        cw * 512:(cw + 1) * 512]), 16)
                            M2[("wld", wi)] = sWL[wi % 2].n
                for src, dst in [
                    (biasFh, biasFt[:]),
                    (biota_i, biotat[:]),
                    (kiota_i, kiotat[:]),
                    (bsel16_i, bsel16t[:]),
                    (lsirep_i, lsit[:]),
                ]:
                    sLD.inc(sync.dma_start(out=dst, in_=src), 16)
                M2["lcst2"] = sLD.n
                W(sync, sCC, M2, ("ccq", 0))
                sST[0].inc(sync.dma_start(
                    out=stq[:], in_=cc_x1oq[0][DH:DH + 2, :]), 16)
                sST[0].inc(sync.dma_start(
                    out=stq2[:],
                    in_=cc_x1oq[0][DH + 2 + DH:2 * (DH + 2), :]), 16)
                M2["lst0"] = sST[0].n
                for c in range(2):
                    sXC[c].inc(sync.dma_start(
                        out=x1pre[:, c, :],
                        in_=cc_x1oq[0][c * 128:(c + 1) * 128, :]), 16)
                    M2[("lx1p", c)] = sXC[c].n

            def b2_te(te):
                def x1stat_mm(pq, pdt):
                    it2 = pq * NDT + pdt
                    W(te, sV, M2, ("x1", pq, pdt))
                    W(te, sS, M2, ("x1q", pq, pdt))
                    if pdt == 0 and pq >= 1:
                        W(te, sS, M2, ("x1st", pq - 1))
                    te.matmul(out=p_x1s[:], lhsT=onescolf[:],
                              rhs=x1t[:, it2 % 2, :],
                              start=pdt == 0, stop=pdt == NDT - 1)
                    inst = te.matmul(out=p_x1q[:], lhsT=onescolbt[:],
                                     rhs=x1sq[:, it2 % 2, :],
                                     start=pdt == 0, stop=pdt == NDT - 1)
                    sT.inc(inst)
                    M2[("tx1", pq, pdt)] = sT.n

                def tdt_mm(q, dt_i):
                    it = q * NDT + dt_i
                    if it >= 2:
                        pit = it - 2
                        W(te, sS, M2, ("sfp", pit // NDT, pit % NDT))
                        W(te, sV, M2, ("dt", pit // NDT, pit % NDT))
                    inst = te.matmul(out=ps_dt[:, it % 2, :],
                                     lhsT=dtwTb[:, dt_i, :],
                                     rhs=xp0b[:, q * TQ:(q + 1) * TQ],
                                     start=True, stop=True)
                    sT.inc(inst)
                    M2[("tdt", q, dt_i)] = sT.n

                for q in range(NQ):
                    tdt_mm(q, 0)
                    tdt_mm(q, 1)
                    if q >= 1:
                        x1stat_mm(q - 1, NDT - 3)
                        x1stat_mm(q - 1, NDT - 2)
                    for j in range(32):
                        if q * 32 + j >= 2:
                            pj = q * 32 + j - 2
                            W(te, sS, M2, ("bc", pj // 32, pj % 32))
                        inst = te.matmul(out=ps_bc[:, j % 2, :],
                                         lhsT=selBCt[:, j * 128:(j + 1) * 128],
                                         rhs=xpBCf[:, q * TQ:(q + 1) * TQ],
                                         start=True, stop=True)
                        sT.inc(inst)
                        M2[("tbc", q, j)] = sT.n
                    if q >= 1:
                        x1stat_mm(q - 1, NDT - 1)
                    for dt_i in range(2, NDT):
                        tdt_mm(q, dt_i)
                        if dt_i >= 3:
                            x1stat_mm(q, dt_i - 3)
                for k in (NDT - 3, NDT - 2, NDT - 1):
                    x1stat_mm(NQ - 1, k)

            def b2_sc(sc):
                def bc_evac(q, j):
                    if q >= 1 and j == 0:
                        W(sc, sV, M2, ("scan", q - 1, NDT - 1, NST - 1))
                    if q >= 1 and j == 16:
                        W(sc, sV, M2, ("ylast", q - 1))
                    W(sc, sT, M2, ("tbc", q, j))
                    dst = BrepA if j < 16 else CrepA
                    inst = sc.activation(out=dst[:, j % 16, :],
                                         in_=ps_bc[:, j % 2, :],
                                         func=AF.Copy)
                    sS.inc(inst)
                    M2[("bc", q, j)] = sS.n

                def sq_cast(it2):
                    q2, d2 = it2 // NDT, it2 % NDT
                    W(sc, sV, M2, ("x1", q2, d2))
                    if it2 >= 2:
                        pit = it2 - 2
                        W(sc, sT, M2, ("tx1", pit // NDT, pit % NDT))
                    inst = sc.activation(out=x1sq[:, it2 % 2, :],
                                         in_=x1t[:, it2 % 2, :],
                                         func=AF.Square)
                    sS.inc(inst)
                    M2[("x1q", q2, d2)] = sS.n

                def x1st_evacs(q2):
                    W(sc, sT, M2, ("tx1", q2, NDT - 1))
                    if q2 >= 1:
                        W(sc, sGD, M2, ("gx1s", q2 - 1))
                    sc.activation(out=x1sumS[:], in_=p_x1s[:], func=AF.Copy)
                    inst = sc.activation(out=x1sqS[:], in_=p_x1q[:],
                                         func=AF.Copy)
                    sS.inc(inst)
                    M2[("x1st", q2)] = sS.n

                for q in range(NQ):
                    for dt_i in range(NDT):
                        it = q * NDT + dt_i
                        W(sc, sT, M2, ("tdt", q, dt_i))
                        if it >= 1:
                            W(sc, sV, M2, ("dt", (it - 1) // NDT,
                                           (it - 1) % NDT))
                        sc.activation(out=sfr[:], in_=ps_dt[:, it % 2, :],
                                      func=AF.Relu,
                                      bias=dtbarr[:, dt_i:dt_i + 1])
                        sc.drain()
                        sc.activation(out=sfu[:], in_=ps_dt[:, it % 2, :],
                                      func=AF.Abs,
                                      bias=dtbarr[:, dt_i:dt_i + 1])
                        sc.drain()
                        sc.activation(out=sfu[:], in_=sfu[:], func=AF.Exp,
                                      scale=-1.0)
                        sc.drain()
                        inst = sc.activation(out=sfl[:], in_=sfu[:],
                                             func=AF.Ln, bias=1.0)
                        sS.inc(inst)
                        M2[("sfp", q, dt_i)] = sS.n
                        for n in range(NST):
                            if dt_i == 0 and n % 4 == 0:
                                for j in range(n, n + 4):
                                    bc_evac(q, j)
                            if n == 0:
                                W(sc, sV, M2, ("dt", q, dt_i))
                            if n >= 2:
                                W(sc, sV, M2, ("scan", q, dt_i, n - 2))
                            elif it >= 1:
                                pit = it - 1
                                W(sc, sV, M2, ("scan", pit // NDT, pit % NDT,
                                               NST - 2 + n))
                            inst = sc.activation(
                                out=dab[:, n % 2, :], in_=dtb2[:, it % 2, :],
                                func=AF.Exp,
                                scale=An[:, dt_i * NST + n:dt_i * NST + n + 1])
                            sS.inc(inst)
                            M2[("exp", q, dt_i, n)] = sS.n
                        if dt_i == 0:
                            for j in range(16, 32):
                                bc_evac(q, j)
                        for wj in range(2):
                            wi = it * 2 + wj
                            W(sc, sWL[wi % 2], M2, ("wld", wi))
                            if wi >= 2:
                                W(sc, sWB[wi % 2], M2, ("wst", wi - 2))
                            inst = sc.activation(out=wtb[:, wi % 2, :],
                                                 in_=wtf[:, wi % 2, :],
                                                 func=AF.Copy)
                            sS.inc(inst)
                            M2[("wc", wi)] = sS.n
                        if it >= 1:
                            sq_cast(it - 1)
                        if dt_i == 0 and q >= 1:
                            x1st_evacs(q - 1)
                sq_cast(NQ * NDT - 1)
                x1st_evacs(NQ - 1)

            def b2_v(v):
                def vsync(inst):
                    val = sVV.inc(inst)
                    if not DRY[0]:
                        v.wait_ge(sVV.h, val)

                inst = v.memset(hbound[:], 0.0)
                vsync(inst)
                for q in range(NQ):
                    qsl = slice(q * TQ, (q + 1) * TQ)
                    for dt_i in range(NDT):
                        it = q * NDT + dt_i
                        W(v, sS, M2, ("sfp", q, dt_i))
                        if it >= 2:
                            pit = it - 2
                            W(v, sS, M2, ("exp", pit // NDT, pit % NDT,
                                          NST - 1))
                        inst = v.tensor_tensor(out=dtb2[:, it % 2, :],
                                               in0=sfr[:], in1=sfl[:],
                                               op=ALU.add)
                        sV.inc(inst)
                        M2[("dt", q, dt_i)] = sV.n
                        W(v, sX[it % 2], M2, ("ldx", q, dt_i))
                        inst = v.tensor_tensor(out=xnb[:], in0=xt2[:, it % 2, :],
                                               in1=mrep[:, qsl], op=ALU.subtract)
                        vsync(inst)
                        inst = v.tensor_tensor(out=xnb[:], in0=xnb[:],
                                               in1=rrep[:, qsl], op=ALU.mult)
                        vsync(inst)
                        inst = v.tensor_tensor(out=dtxb[:],
                                               in0=dtb2[:, it % 2, :],
                                               in1=xnb[:], op=ALU.mult)
                        vsync(inst)
                        dtxb4 = dtxb[:].rearrange(
                            "p (o f) -> p o f", o=1).broadcast_to((128, 4, TQ))
                        for grp in range(4):
                            if dt_i == 0:
                                W(v, sS, M2, ("bc", q, grp * 4 + 3))
                            inst = v.tensor_tensor(
                                out=dbxh[:],
                                in0=dtxb4,
                                in1=BrepA[:, grp * 4:(grp + 1) * 4, :],
                                op=ALU.mult)
                            vsync(inst)
                            for nn in range(4):
                                n = grp * 4 + nn
                                W(v, sS, M2, ("exp", q, dt_i, n))
                                slot = dt_i * NST + n
                                inst = v.tensor_tensor_scan(
                                    out=hall[:, n, :],
                                    data0=dab[:, n % 2, :],
                                    data1=dbxh[:, nn, :],
                                    initial=hbound[:, slot:slot + 1],
                                    op0=ALU.mult, op1=ALU.add)
                                sV.inc(inst)
                                M2[("scan", q, dt_i, n)] = sV.n
                            if not DRY[0]:
                                v.wait_ge(sV.h, M2[("scan", q, dt_i,
                                                    grp * 4 + 3)])
                        inst = v.tensor_copy(
                            out=hbound[:, dt_i * NST:(dt_i + 1) * NST],
                            in_=hall[:, :, TQ - 1:TQ].rearrange(
                                "p n o -> p (n o)"))
                        vsync(inst)
                        if dt_i == 0:
                            W(v, sS, M2, ("bc", q, 31))
                        inst = v.tensor_tensor(out=hall[:], in0=hall[:],
                                               in1=CrepA[:], op=ALU.mult)
                        vsync(inst)
                        for lvl in (8, 4, 2):
                            inst = v.tensor_tensor(
                                out=hall[:, 0:lvl, :],
                                in0=hall[:, 0:lvl, :],
                                in1=hall[:, lvl:2 * lvl, :],
                                op=ALU.add)
                            vsync(inst)
                        inst = v.tensor_tensor(out=yac[:], in0=hall[:, 0, :],
                                               in1=hall[:, 1, :], op=ALU.add)
                        vsync(inst)
                        if it >= 2:
                            pit = it - 2
                            W(v, sY[pit % 2], M2, ("gx1", pit // NDT, pit % NDT))
                            W(v, sT, M2, ("tx1", pit // NDT, pit % NDT))
                        inst = v.scalar_tensor_tensor(out=hcb[:], in0=xnb[:],
                                                      scalar=Darr[:, dt_i:dt_i + 1],
                                                      in1=yac[:],
                                                      op0=ALU.mult, op1=ALU.add)
                        vsync(inst)
                        inst = v.tensor_tensor(out=x1t[:, it % 2, :],
                                               in0=hcb[:],
                                               in1=xt2[:, it % 2, :],
                                               op=ALU.add)
                        val = sV.inc(inst)
                        M2[("x1", q, dt_i)] = val
                        if not DRY[0]:
                            v.wait_ge(sV.h, val)
                    inst = v.tensor_copy(out=hcb[:, 0:1], in_=hcb[:, 0:1])
                    sV.inc(inst)
                    M2[("ylast", q)] = sV.n

            def b2_g(g):
                for q in range(NQ):
                    for dt_i in range(NDT):
                        it = q * NDT + dt_i
                        W(g, sV, M2, ("x1", q, dt_i))
                        sY[it % 2].inc(g.dma_start(
                            out=cc_x1q[q][dt_i * 128:(dt_i + 1) * 128, :],
                            in_=x1t[:, it % 2, :]), 16)
                        M2[("gx1", q, dt_i)] = sY[it % 2].n
                        for wj in range(2):
                            wi = it * 2 + wj
                            rw, cw = wi // 4, wi % 4
                            W(g, sS, M2, ("wc", wi))
                            sWB[wi % 2].inc(g.dma_start(
                                out=wbf[rw * 128:(rw + 1) * 128,
                                        cw * 512:(cw + 1) * 512],
                                in_=wtb[:, wi % 2, :]), 16)
                            M2[("wst", wi)] = sWB[wi % 2].n
                    W(g, sS, M2, ("x1st", q))
                    sGD.inc(g.dma_start(out=cc_x1q[q][DH:DH + 1, :],
                                        in_=x1sumS[:]), 16)
                    sGD.inc(g.dma_start(out=cc_x1q[q][DH + 1:DH + 2, :],
                                        in_=x1sqS[:]), 16)
                    M2[("gx1s", q)] = sGD.n
                    M2[("yq0", q)] = sY[0].n
                    M2[("yq1", q)] = sY[1].n
                    W(g, sGD, M2, ("gx1s", q))
                    W(g, sY[0], M2, ("yq0", q))
                    W(g, sY[1], M2, ("yq1", q))
                    sCC.inc(g.collective_compute(
                        "AllGather", ALU.bypass,
                        replica_groups=[[0, 1], [2, 3], [4, 5], [6, 7]],
                        ins=[cc_x1q[q].opt()], outs=[cc_x1oq[q].opt()]))
                    M2[("ccq", q)] = sCC.n
                M2["cc2"] = sCC.n
                M2["wstot0"] = sWB[0].n
                M2["wstot1"] = sWB[1].n
                W(g, sWB[0], M2, "wstot0")
                W(g, sWB[1], M2, "wstot1")

            run_block([("sync", b2_sync), ("tensor", b2_te),
                       ("scalar", b2_sc), ("vector", b2_v),
                       ("gpsimd", b2_g)], blk2)

        es12.close()
        _es3 = ExitStack()
        with _es3:
            e3 = _es3.enter_context
            xn2b = e3(nc.sbuf_tensor("xn2b", [128, NCH, L], BF16))
            x1ch = e3(nc.sbuf_tensor("x1ch", [128, 4, TS], F32))
            vrq = e3(nc.sbuf_tensor("vrq", [1, TS], F32))
            rsq = e3(nc.sbuf_tensor("rsq", [1, TS], F32))
            mrep2 = e3(nc.sbuf_tensor("mrep2", [128, 2, TS], F32))
            rrep2 = e3(nc.sbuf_tensor("rrep2", [128, 2, TS], F32))
            wslb = e3(nc.sbuf_tensor("wslb", [128, 2, NCH, 128], BF16))
            latf1 = e3(nc.sbuf_tensor("latf1", [128, 4, 48], F32))
            lATt = e3(nc.sbuf_tensor("lATt", [128, NCH, 48], BF16))
            lBT1 = e3(nc.sbuf_tensor("lBT1", [16, DH], F32))
            l16buf = e3(nc.sbuf_tensor("l16buf", [16, DH], F32))
            lBeb = e3(nc.sbuf_tensor("lBeb", [16, DH], BF16))
            gt1 = e3(nc.sbuf_tensor("gt1", [4, 4], F32))
            gt2 = e3(nc.sbuf_tensor("gt2", [4, 1], F32))
            gt3 = e3(nc.sbuf_tensor("gt3", [4, 1], F32))
            gt4 = e3(nc.sbuf_tensor("gt4", [4, 3], F32))
            wk16 = e3(nc.sbuf_tensor("wk16", [16, 3], F32))
            rselb = e3(nc.sbuf_tensor("rselb", [16, L], BF16))
            x1r = e3(nc.sbuf_tensor("x1r", [128, 2, TS], F32))
            x2t = e3(nc.sbuf_tensor("x2t", [128, TS], F32))
            p_br2 = e3(nc.psum_tensor("p_br2", [128, 2, TS], F32))
            p_rk = e3(nc.psum_tensor("p_rk", [16, TS], F32))
            p_wk = e3(nc.psum_tensor("p_wk", [16, 3], F32))
            p_mm = e3(nc.psum_tensor("p_mm", [128, 2, TS], F32))
            blk3 = e3(nc.Block())
            M3 = {}

            def orow(q, rank):
                return rank * (DH + 2)

            lBef = l16buf[:, 0:DH]

            def rkt(k):
                return l16buf[:, k * TS:(k + 1) * TS]

            rself = l16buf[:, 3 * TS:4 * TS]

            def b3_g(g):
                def gvr_fetch(ts_i):
                    W(g, sV, M3, ("vf1", ts_i))
                    sGD.inc(g.dma_start(out=vrq[:], in_=stq[1:2, :]), 16)
                    M3[("gvr", ts_i)] = sGD.n

                gvr_fetch(0)
                for ot in range(2):
                    for c4 in range(4):
                        rs = c4 * 1024
                        inst = g.dma_start(
                            out=wslb[:, ot % 2, c4 * 8:(c4 + 1) * 8, :],
                            in_=wbf[rs:rs + 1024,
                                    ot * 128:(ot + 1) * 128].rearrange(
                                        "(c p) o -> p c o", p=128))
                        sWs[ot % 2].inc(inst, 16)
                    M3[("lw", ot)] = sWs[ot % 2].n
                for ts_i in range(1, NTS):
                    gvr_fetch(ts_i)
                for ot in range(NOT):
                    for ts_i in range(NTS):
                        W(g, sV, M3, ("x2", ot, ts_i))
                        sGD.inc(g.dma_start(
                            out=oq[ot * 128:(ot + 1) * 128,
                                   ts_i * TS:(ts_i + 1) * TS],
                            in_=x2t[:]), 16)
                        M3[("go", ot, ts_i)] = sGD.n
                    if ot + 2 < NOT:
                        po = ot + 2
                        W(g, sT, M3, ("tmm", ot, NTS - 1))
                        for c4 in range(4):
                            rs = c4 * 1024
                            inst = g.dma_start(
                                out=wslb[:, po % 2, c4 * 8:(c4 + 1) * 8, :],
                                in_=wbf[rs:rs + 1024,
                                        po * 128:(po + 1) * 128].rearrange(
                                            "(c p) o -> p c o", p=128))
                            sWs[po % 2].inc(inst, 16)
                        M3[("lw", po)] = sWs[po % 2].n
                W(g, sGD, M3, ("go", NOT - 1, NTS - 1))

            def b3_sync(sync):
                def stats_load(ts_i):
                    if ts_i >= 1:
                        W(sync, sT, M3, ("tbr2", ts_i - 1))
                    W(sync, sCC, M2, ("ccq", ts_i))
                    r0 = orow(ts_i, 0)
                    r1 = orow(ts_i, 1)
                    sST[ts_i % 2].inc(sync.dma_start(
                        out=stq[:],
                        in_=cc_x1oq[ts_i][r0 + DH:r0 + DH + 2, :]), 16)
                    sST[ts_i % 2].inc(sync.dma_start(
                        out=stq2[:],
                        in_=cc_x1oq[ts_i][r1 + DH:r1 + DH + 2, :]), 16)
                    M3[("lst", ts_i)] = sST[ts_i % 2].n

                def x1_loads(ts_i, cs=0, ce=NCH):
                    r0 = orow(ts_i, 0)
                    r1 = orow(ts_i, 1)
                    for c in range(cs, ce):
                        k = ts_i * NCH + c
                        if k >= 4:
                            pk = k - 4
                            W(sync, sV, M3, ("xn2", pk // NCH, pk % NCH))
                        src_row = (r0 + c * 128 if c < NDT
                                   else r1 + (c - NDT) * 128)
                        sXC[k % 4].inc(sync.dma_start(
                            out=x1ch[:, k % 4, :],
                            in_=cc_x1oq[ts_i][src_row:src_row + 128, :]), 16)
                        M3[("lx1", ts_i, c)] = sXC[k % 4].n

                def lbt_load(k):
                    if k >= 1:
                        W(sync, sV, M3, ("vlb", k - 1))
                    sLB.inc(sync.dma_start(
                        out=lBT1[:], in_=lBTh_i[:, k * DH:(k + 1) * DH]), 16)
                    M3[("llb", k)] = sLB.n

                x1_loads(0, 2, 8)
                lbt_load(0)
                x1_loads(0, 8, 20)
                stats_load(1)
                x1_loads(0, 20, NCH)
                for c in range(NCH):
                    if c >= 4:
                        W(sync, sS, M3, ("lat", c - 4))
                    sLA[c % 4].inc(sync.dma_start(
                        out=latf1[:, c % 4, :],
                        in_=lATf_i[c * 128:(c + 1) * 128, :]), 16)
                    M3[("llat", c)] = sLA[c % 4].n
                lbt_load(1)
                lbt_load(2)
                stats_load(2)
                x1_loads(1)
                stats_load(3)
                x1_loads(2)
                x1_loads(3)
                for ot in range(NOT):
                    for ts_i in range(NTS):
                        k = ot * NTS + ts_i
                        if k >= 2:
                            pk = k - 2
                            W(sync, sV, M3, ("x2", pk // NTS, pk % NTS))
                        sR[k % 2].inc(sync.dma_start(
                            out=x1r[:, k % 2, :],
                            in_=cc_x1q[ts_i][ot * 128:(ot + 1) * 128,
                                             :]), 16)
                        M3[("lx1r", ot, ts_i)] = sR[k % 2].n

            def b3_te1(te):
                def trk_mms(ts_i):
                    sl = slice(ts_i * TS, (ts_i + 1) * TS)
                    if ts_i == 0:
                        W(te, sS, M3, ("lat", NCH - 1))
                    W(te, sV, M3, ("xn2", ts_i, NCH - 1))
                    for k in range(3):
                        if ts_i * 3 + k >= 1:
                            pk = ts_i * 3 + k - 1
                            W(te, sS, M3, ("rk", pk // 3, pk % 3))
                        for c in range(NCH):
                            inst = te.matmul(
                                out=p_rk[:],
                                lhsT=lATt[:, c, k * 16:(k + 1) * 16],
                                rhs=xn2b[:, c, sl],
                                start=c == 0, stop=c == NCH - 1)
                        sT.inc(inst)
                        M3[("trk", ts_i, k)] = sT.n

                def tbr2_mms(ts_i):
                    if ts_i >= 1:
                        W(te, sS, M3, ("br2", ts_i - 1))
                    W(te, sS, M3, ("srs", ts_i))
                    te.matmul(out=p_br2[:, 0, :], lhsT=onesrowt[:],
                              rhs=stq[0:1, :], start=True, stop=True)
                    inst = te.matmul(out=p_br2[:, 1, :], lhsT=onesrowt[:],
                                     rhs=rsq[:], start=True, stop=True)
                    sT.inc(inst)
                    M3[("tbr2", ts_i)] = sT.n

                tbr2_mms(0)
                W(te, sV, M3, "vgate")
                inst = te.matmul(out=p_wk[:], lhsT=bsel16t[:],
                                 rhs=gt4[:], start=True, stop=True)
                sT.inc(inst)
                M3["twk"] = sT.n
                tbr2_mms(1)
                trk_mms(0)
                tbr2_mms(2)
                trk_mms(1)
                tbr2_mms(3)
                trk_mms(2)
                trk_mms(3)

            def b3_s3(sc):
                def chain_s(ts_i):
                    W(sc, sV, M3, ("vvar", ts_i))
                    inst = sc.activation(out=vrq[:], in_=vrq[:], func=AF.Ln,
                                         bias=epscol[0:1, :])
                    sS.inc(inst)
                    M3[("sln", ts_i)] = sS.n
                    W(sc, sV, M3, ("vnh", ts_i))
                    inst = sc.activation(out=rsq[:], in_=vrq[:], func=AF.Exp)
                    sS.inc(inst)
                    M3[("srs", ts_i)] = sS.n
                    W(sc, sT, M3, ("tbr2", ts_i))
                    if ts_i >= 2:
                        W(sc, sV, M3, ("xn2", ts_i - 2, NCH - 1))
                    sc.activation(out=mrep2[:, ts_i % 2, :],
                                  in_=p_br2[:, 0, :], func=AF.Copy)
                    inst = sc.activation(out=rrep2[:, ts_i % 2, :],
                                         in_=p_br2[:, 1, :], func=AF.Copy)
                    sS.inc(inst)
                    M3[("br2", ts_i)] = sS.n

                def rk_evacs(ts_i):
                    for k in range(3):
                        W(sc, sT, M3, ("trk", ts_i, k))
                        if ts_i == 0 and k == 0:
                            W(sc, sV, M3, "vlbe")
                        if ts_i >= 1:
                            W(sc, sV, M3, ("rsel", ts_i - 1))
                        inst = sc.activation(out=rkt(k), in_=p_rk[:],
                                             func=AF.Copy)
                        sS.inc(inst)
                        M3[("rk", ts_i, k)] = sS.n

                chain_s(0)
                W(sc, sT, M3, "twk")
                inst = sc.activation(out=wk16[:], in_=p_wk[:], func=AF.Copy)
                sS.inc(inst)
                M3["swk"] = sS.n
                chain_s(1)
                for c in range(NCH):
                    W(sc, sLA[c % 4], M3, ("llat", c))
                    inst = sc.activation(out=lATt[:, c, :],
                                         in_=latf1[:, c % 4, :],
                                         func=AF.Copy)
                    sS.inc(inst)
                    M3[("lat", c)] = sS.n
                rk_evacs(0)
                chain_s(2)
                rk_evacs(1)
                chain_s(3)
                rk_evacs(2)
                rk_evacs(3)

            def b3_v3(v):
                def vsync3(inst):
                    val = sVV.inc(inst)
                    if not DRY[0]:
                        v.wait_ge(sVV.h, val)

                def gate_p1():
                    W(v, sLD, M2, "lcst2")
                    inst = v.tensor_copy(out=lsif[:], in_=lsit[:])
                    vsync3(inst)
                    inst = v.tensor_tensor(out=gt1[:], in0=biotat[:],
                                           in1=lsif[:], op=ALU.is_lt)
                    vsync3(inst)
                    inst = v.tensor_scalar(out=gt2[:], in0=gt1[:, 0:1],
                                           scalar1=-1.0, scalar2=1.0,
                                           op0=ALU.mult, op1=ALU.add)
                    vsync3(inst)
                    inst = v.tensor_tensor(out=gt2[:], in0=gt2[:],
                                           in1=gt1[:, 1:2], op=ALU.mult)
                    vsync3(inst)
                    inst = v.tensor_scalar(out=gt3[:], in0=gt1[:, 1:2],
                                           scalar1=-2.0, scalar2=2.0,
                                           op0=ALU.mult, op1=ALU.add)
                    vsync3(inst)

                def gate_p2():
                    inst = v.tensor_tensor(out=gt2[:], in0=gt2[:], in1=gt3[:],
                                           op=ALU.add)
                    vsync3(inst)
                    inst = v.tensor_scalar(out=gt3[:], in0=gt1[:, 3:4],
                                           scalar1=-1.0, scalar2=4.0,
                                           op0=ALU.mult, op1=ALU.add)
                    vsync3(inst)
                    inst = v.tensor_tensor(out=gt2[:], in0=gt2[:], in1=gt3[:],
                                           op=ALU.subtract)
                    vsync3(inst)
                    inst = v.tensor_tensor(out=gt2[:], in0=gt2[:],
                                           in1=gt1[:, 2:3], op=ALU.mult)
                    vsync3(inst)
                    inst = v.tensor_tensor(out=gt2[:], in0=gt2[:], in1=gt3[:],
                                           op=ALU.add)
                    vsync3(inst)

                def gate_p3():
                    inst = v.tensor_scalar(out=gt3[:], in0=gt2[:],
                                           scalar1=-2.0, scalar2=0.0,
                                           op0=ALU.add, op1=ALU.max)
                    vsync3(inst)
                    inst = v.tensor_scalar(out=gt3[:], in0=gt3[:], scalar1=2.0,
                                           scalar2=None, op0=ALU.min)
                    vsync3(inst)
                    inst = v.tensor_scalar(out=gt2[:], in0=gt2[:], scalar1=2.0,
                                           scalar2=None, op0=ALU.is_ge)
                    vsync3(inst)
                    inst = v.tensor_scalar(out=gt4[:], in0=kiotat[:],
                                           scalar1=gt3[:], scalar2=None,
                                           op0=ALU.is_equal)
                    vsync3(inst)
                    inst = v.tensor_scalar(out=gt4[:], in0=gt4[:],
                                           scalar1=gt2[:], scalar2=None,
                                           op0=ALU.mult)
                    sV.inc(inst)
                    M3["vgate"] = sV.n

                def chain_a(ts_i):
                    if ts_i >= 1:
                        W(v, sT, M3, ("tbr2", ts_i - 1))
                        W(v, sST[ts_i % 2], M3, ("lst", ts_i))
                    else:
                        W(v, sST[0], M2, "lst0")
                    inst = v.tensor_tensor(out=stq[:], in0=stq[:],
                                           in1=stq2[:], op=ALU.add)
                    vsync3(inst)
                    inst = v.tensor_scalar(out=stq[:], in0=stq[:],
                                           scalar1=1.0 / D, scalar2=None,
                                           op0=ALU.mult)
                    sV.inc(inst)
                    M3[("vf1", ts_i)] = sV.n

                def chain_b(ts_i):
                    W(v, sGD, M3, ("gvr", ts_i))
                    inst = v.tensor_tensor(out=stq2[0:1, :], in0=stq[0:1, :],
                                           in1=stq[0:1, :], op=ALU.mult)
                    vsync3(inst)
                    inst = v.tensor_tensor(out=vrq[:], in0=vrq[:],
                                           in1=stq2[0:1, :], op=ALU.subtract)
                    sV.inc(inst)
                    M3[("vvar", ts_i)] = sV.n

                def chain_c(ts_i):
                    W(v, sS, M3, ("sln", ts_i))
                    inst = v.tensor_scalar(out=vrq[:], in0=vrq[:],
                                           scalar1=-0.5, scalar2=None,
                                           op0=ALU.mult)
                    sV.inc(inst)
                    M3[("vnh", ts_i)] = sV.n

                def lbeff(k):
                    if k == 0:
                        W(v, sS, M3, "swk")
                    W(v, sLB, M3, ("llb", k))
                    if k == 0:
                        inst = v.tensor_scalar(out=lBef, in0=lBT1[:],
                                               scalar1=wk16[:, 0:1],
                                               scalar2=None, op0=ALU.mult)
                    else:
                        inst = v.scalar_tensor_tensor(out=lBef, in0=lBT1[:],
                                                      scalar=wk16[:, k:k + 1],
                                                      in1=lBef,
                                                      op0=ALU.mult,
                                                      op1=ALU.add)
                    vsync3(inst)
                    sV.inc(v.tensor_copy(out=gt1[:, 0:1], in_=gt1[:, 0:1]))
                    M3[("vlb", k)] = sV.n

                def lbeb():
                    inst = v.tensor_copy(out=lBeb[:], in_=lBef)
                    sV.inc(inst)
                    M3["vlbe"] = sV.n

                def rsel(ts_i):
                    sl = slice(ts_i * TS, (ts_i + 1) * TS)
                    W(v, sS, M3, ("rk", ts_i, 2))
                    inst = v.tensor_scalar(out=rself, in0=rkt(0),
                                           scalar1=wk16[:, 0:1], scalar2=None,
                                           op0=ALU.mult)
                    for k in (1, 2):
                        vsync3(inst)
                        inst = v.scalar_tensor_tensor(out=rself, in0=rkt(k),
                                                      scalar=wk16[:, k:k + 1],
                                                      in1=rself,
                                                      op0=ALU.mult,
                                                      op1=ALU.add)
                    vsync3(inst)
                    inst = v.tensor_copy(out=rselb[:, sl], in_=rself)
                    sV.inc(inst)
                    M3[("rsel", ts_i)] = sV.n

                gate_p1()
                chain_a(0)
                gate_p2()
                chain_b(0)
                gate_p3()
                chain_c(0)

                fillers = {}

                def add_f(ts_i, c, fn):
                    fillers.setdefault((ts_i, c), []).append(fn)

                add_f(0, 20, lambda: chain_a(1))
                add_f(0, 23, lambda: chain_b(1))
                add_f(0, 26, lambda: chain_c(1))
                add_f(0, 27, lambda: lbeff(0))
                add_f(0, 28, lambda: lbeff(1))
                add_f(0, 29, lambda: lbeff(2))
                add_f(0, 30, lbeb)
                for ts_i in (1, 2):
                    add_f(ts_i, 10, (lambda t: lambda: rsel(t))(ts_i - 1))
                    add_f(ts_i, 12, (lambda t: lambda: chain_a(t))(ts_i + 1))
                    add_f(ts_i, 18, (lambda t: lambda: chain_b(t))(ts_i + 1))
                    add_f(ts_i, 24, (lambda t: lambda: chain_c(t))(ts_i + 1))
                add_f(3, 10, lambda: rsel(2))

                for ts_i in range(NTS):
                    sl = slice(ts_i * TS, (ts_i + 1) * TS)
                    for c in range(NCH):
                        k = ts_i * NCH + c
                        if ts_i == 0 and c < 2:
                            W(v, sXC[c], M2, ("lx1p", c))
                            buf = x1pre[:, c, :]
                        else:
                            W(v, sXC[k % 4], M3, ("lx1", ts_i, c))
                            buf = x1ch[:, k % 4, :]
                        if c == 0:
                            W(v, sS, M3, ("br2", ts_i))
                        inst = v.tensor_tensor(out=buf,
                                               in0=buf,
                                               in1=mrep2[:, ts_i % 2, :],
                                               op=ALU.subtract)
                        vsync3(inst)
                        inst = v.tensor_tensor(out=xn2b[:, c, sl],
                                               in0=buf,
                                               in1=rrep2[:, ts_i % 2, :],
                                               op=ALU.mult)
                        sV.inc(inst)
                        M3[("xn2", ts_i, c)] = sV.n
                        for f in fillers.get((ts_i, c), []):
                            f()
                rsel(3)

            def b3_te2(te):
                for ot in range(NOT):
                    W(te, sWs[ot % 2], M3, ("lw", ot))
                    for ts_i in range(NTS):
                        k = ot * NTS + ts_i
                        sl = slice(ts_i * TS, (ts_i + 1) * TS)
                        if ot == 0:
                            W(te, sV, M3, ("xn2", ts_i, NCH - 1))
                            W(te, sV, M3, ("rsel", ts_i))
                        if k >= 2:
                            pk = k - 2
                            W(te, sS, M3, ("x2s", pk // NTS, pk % NTS))
                        for c in range(NCH):
                            te.matmul(out=p_mm[:, k % 2, :],
                                      lhsT=wslb[:, ot % 2, c, :],
                                      rhs=xn2b[:, c, sl],
                                      start=c == 0, stop=False)
                        inst = te.matmul(out=p_mm[:, k % 2, :],
                                         lhsT=lBeb[:, ot * 128:(ot + 1) * 128],
                                         rhs=rselb[:, sl],
                                         start=False, stop=True)
                        sT.inc(inst)
                        M3[("tmm", ot, ts_i)] = sT.n

            def b3_s4(sc):
                for ot in range(NOT):
                    for ts_i in range(NTS):
                        k = ot * NTS + ts_i
                        W(sc, sT, M3, ("tmm", ot, ts_i))
                        if k >= 1:
                            pk = k - 1
                            W(sc, sGD, M3, ("go", pk // NTS, pk % NTS))
                        inst = sc.activation(out=x2t[:],
                                             in_=p_mm[:, k % 2, :],
                                             func=AF.Identity,
                                             bias=biasFt[:, ot:ot + 1])
                        sS.inc(inst)
                        M3[("x2s", ot, ts_i)] = sS.n

            def b3_v4(v):
                for ot in range(NOT):
                    for ts_i in range(NTS):
                        k = ot * NTS + ts_i
                        W(v, sS, M3, ("x2s", ot, ts_i))
                        W(v, sR[k % 2], M3, ("lx1r", ot, ts_i))
                        if k >= 2:
                            pk = k - 2
                            W(v, sGD, M3, ("go", pk // NTS, pk % NTS))
                        inst = v.tensor_tensor(out=x2t[:],
                                               in0=x2t[:],
                                               in1=x1r[:, k % 2, :],
                                               op=ALU.add)
                        sV.inc(inst)
                        M3[("x2", ot, ts_i)] = sV.n

            run_block([("gpsimd", b3_g), ("sync", b3_sync),
                       ("tensor", b3_te1), ("scalar", b3_s3),
                       ("vector", b3_v3), ("tensor", b3_te2),
                       ("scalar", b3_s4), ("vector", b3_v4)], blk3)
            import os as _os
            if _os.environ.get("DUMP_M3"):
                import json as _json
                _json.dump({str(k): v for k, v in M3.items()},
                           open("/tmp/m3.json", "w"))
                _json.dump({str(k): v for k, v in M2.items()},
                           open("/tmp/m2.json", "w"))

    return nc


_NC_CACHE = None


def kernel(L0_attn_x, L0_attn_A_log, L0_attn_D, L0_attn_x_proj_w,
           L0_attn_dt_proj_w, L0_attn_dt_proj_b, L0_ffn_weight, L0_ffn_bias,
           L0_ffn_lora_A, L0_ffn_lora_B, L0_ffn_lora_start_indices):
    global _NC_CACHE
    import ml_dtypes
    x = np.ascontiguousarray(np.asarray(L0_attn_x, dtype=np.float32))
    A_log = np.asarray(L0_attn_A_log, dtype=np.float32)
    Dv = np.asarray(L0_attn_D, dtype=np.float32)
    xpw = np.asarray(L0_attn_x_proj_w, dtype=np.float32)
    dtw = np.asarray(L0_attn_dt_proj_w, dtype=np.float32)
    dtb = np.asarray(L0_attn_dt_proj_b, dtype=np.float32)
    Wm = np.asarray(L0_ffn_weight, dtype=np.float32)
    biasF = np.asarray(L0_ffn_bias, dtype=np.float32)
    lA = np.asarray(L0_ffn_lora_A, dtype=np.float32)
    lB = np.asarray(L0_ffn_lora_B, dtype=np.float32)
    lsi = np.asarray(L0_ffn_lora_start_indices)

    WT = np.ascontiguousarray(Wm.T)
    xpwTm = np.ascontiguousarray(xpw.T)
    dtwTm = np.ascontiguousarray(dtw.T)
    lATf = np.ascontiguousarray(lA.transpose(2, 0, 1).reshape(D, 48))
    lBT = lB.transpose(0, 2, 1)

    selBC = np.zeros((32, 32 * 128), np.float32)
    for j in range(32):
        selBC[j, j * 128:(j + 1) * 128] = 1.0
    onesrow = np.ones((1, 128), np.float32)
    onescol_b = np.ones((128, 1), np.float32).astype(ml_dtypes.bfloat16)
    biota = np.broadcast_to(np.arange(4, dtype=np.float32)[:, None],
                            (4, 4)).copy()
    kiota = np.broadcast_to(np.arange(3, dtype=np.float32)[None, :],
                            (4, 3)).copy()
    lsirep = np.broadcast_to(lsi.astype(np.int32)[None, :], (4, 4)).copy()

    in_maps = []
    for c in range(8):
        b, h = c // 2, c % 2
        sl = slice(h * DH, (h + 1) * DH)
        bsel16 = np.zeros((4, 16), np.float32)
        bsel16[b, :] = 1.0
        in_maps.append({
            "xTd": np.ascontiguousarray(x[b].T[sl]),
            "xpwT": np.ascontiguousarray(xpwTm[sl]),
            "dtwT": np.ascontiguousarray(dtwTm[:, sl]),
            "Aarr": np.ascontiguousarray(
                A_log[sl].reshape(NDT, 128, NST).transpose(1, 0, 2)
                .reshape(128, NDT * NST)),
            "Darr": np.ascontiguousarray(Dv[sl].reshape(NDT, 128).T),
            "dtbarr": np.ascontiguousarray(dtb[sl].reshape(NDT, 128).T),
            "WTh": np.ascontiguousarray(WT[:, sl]),
            "biasFh": np.ascontiguousarray(biasF[sl].reshape(NOT, 128).T),
            "lATf": lATf,
            "lBTh": np.ascontiguousarray(
                lBT[:, :, sl].transpose(1, 0, 2).reshape(16, 3 * DH)),
            "lsirep": lsirep,
            "biota": biota,
            "kiota": kiota,
            "bsel16": bsel16,
            "selBC": selBC,
            "onesrow": onesrow,
            "onescol_b": onescol_b,
        })

    if _NC_CACHE is None:
        _NC_CACHE = build_nc()
    res = run_bass_kernel_spmd(_NC_CACHE, in_maps, core_ids=list(range(8)))

    out = np.empty((B, L, D), np.float32)
    for c in range(8):
        b, h = c // 2, c % 2
        out[b, :, h * DH:(h + 1) * DH] = res.results[c]["oq"].T
    return out


# version 23 (final)
# speedup vs baseline: 1.1847x; 1.1847x over previous
import numpy as np
from contextlib import ExitStack

import concourse.bass as bass
import concourse.mybir as mybir
from concourse.bass_utils import run_bass_kernel_spmd

F32 = mybir.dt.float32
BF16 = mybir.dt.bfloat16
I32 = mybir.dt.int32
AF = mybir.ActivationFunctionType
ALU = mybir.AluOpType

B, L, D = 4, 2048, 4096
DH = D // 2
NST = 16
RK = 128
NDT = DH // 128
TQ = 512
NQ = L // TQ
EPS = 1e-5
TS = 512
NTS = L // TS
NCH = D // 128
NOT = DH // 128

DRY = [False]


class _Dummy:
    def __getattr__(self, name):
        return lambda *a, **kw: self

    def __call__(self, *a, **kw):
        return self


class Sem:
    def __init__(self, es, nc, name):
        self.h = es.enter_context(nc.semaphore(name))
        self.n = 0

    def inc(self, inst, k=1):
        self.n += k
        if not DRY[0]:
            inst.then_inc(self.h, k)
        return self.n


def W(eng, sem, val_or_dict, key=None):
    if DRY[0]:
        return
    v = val_or_dict if key is None else val_or_dict[key]
    eng.wait_ge(sem.h, v)


def build_nc():
    nc = bass.Bass("TRN2", target_bir_lowering=False, debug=False)

    def din(name, shape, dt=F32):
        return nc.dram_tensor(name, shape, dt, kind="ExternalInput").ap()

    xTd = din("xTd", [DH, L])
    xpwT = din("xpwT", [DH, 160])
    dtwT = din("dtwT", [RK, DH])
    Aarr_i = din("Aarr", [128, NDT * NST])
    Darr_i = din("Darr", [128, NDT])
    dtbarr_i = din("dtbarr", [128, NDT])
    WTh = din("WTh", [D, DH])
    biasFh = din("biasFh", [128, NOT])
    lATf_i = din("lATf", [D, 48])
    lBTh_i = din("lBTh", [16, 3 * DH])
    lsirep_i = din("lsirep", [4, 4], I32)
    biota_i = din("biota", [4, 4])
    kiota_i = din("kiota", [4, 3])
    bsel16_i = din("bsel16", [4, 16])
    selBC_i = din("selBC", [32, 32 * 128])
    onesrow_i = din("onesrow", [1, 128])
    onescol_b_i = din("onescol_b", [128, 1], BF16)

    oq = nc.dram_tensor("oq", [DH, L], F32, kind="ExternalOutput").ap()

    cc_sxq = [nc.dram_tensor(f"cc_sxq{q}", [162, TQ], F32).ap()
              for q in range(NQ)]
    cc_sxoq = [nc.dram_tensor(f"cc_sxoq{q}", [162, TQ], F32).ap()
               for q in range(NQ)]
    cc_x1q = [nc.dram_tensor(f"cc_x1q{q}", [DH + 2, TQ], F32).ap()
              for q in range(NQ)]
    cc_x1oq = [nc.dram_tensor(f"cc_x1oq{q}", [2 * (DH + 2), TQ], F32).ap()
               for q in range(NQ)]
    wbf = nc.dram_tensor("wbf", [D, DH], BF16).ap()

    es = ExitStack()
    with es:
        def sb(name, shape, dt=F32):
            return es.enter_context(nc.sbuf_tensor(name, shape, dt))

        onesrowt = sb("onesrowt", [1, 128])
        onescolbt = sb("onescolbt", [128, 1], BF16)
        onescolf = sb("onescolf", [128, 1])
        epscol = sb("epscol", [128, 1])
        stq = es.enter_context(nc.sbuf_tensor("stq", [2, TQ], F32))
        stq2 = es.enter_context(nc.sbuf_tensor("stq2", [2, TQ], F32))
        biasFt = sb("biasFt", [128, NOT])
        lsit = sb("lsit", [4, 4], I32)
        lsif = sb("lsif", [4, 4])
        biotat = sb("biotat", [4, 4])
        kiotat = sb("kiotat", [4, 3])
        bsel16t = sb("bsel16t", [4, 16])
        es12 = ExitStack()

        def sb12(name, shape, dt=F32):
            return es12.enter_context(nc.sbuf_tensor(name, shape, dt,
                                                     side="right"))

        dtwTb = sb12("dtwTb", [128, NDT, 128], BF16)
        An = sb12("An", [128, NDT * NST])
        Darr = sb12("Darr_s", [128, NDT])
        dtbarr = sb12("dtbarr_s", [128, NDT])
        selBCt = sb12("selBCt", [32, 32 * 128])
        mrep = sb12("mrep", [128, L])
        rrep = sb12("rrep", [128, L])
        xp0b = sb12("xp0b", [128, L], BF16)
        xpBCf = sb12("xpBCf", [32, L])

        sLD = Sem(es, nc, "sLD")
        sGD = Sem(es, nc, "sGD")
        sCC = Sem(es, nc, "sCC")
        sT = Sem(es, nc, "sT")
        sS = Sem(es, nc, "sS")
        sV = Sem(es, nc, "sV")
        sX = [Sem(es, nc, "sX0"), Sem(es, nc, "sX1")]
        sY = [Sem(es, nc, "sY0"), Sem(es, nc, "sY1")]
        sXC = [Sem(es, nc, f"sXC{i}") for i in range(6)]
        sWs = [Sem(es, nc, "sW0"), Sem(es, nc, "sW1")]
        sR = [Sem(es, nc, "sR0"), Sem(es, nc, "sR1")]
        sLA = [Sem(es, nc, f"sLA{i}") for i in range(4)]
        sLB = Sem(es, nc, "sLB")
        sVV = Sem(es, nc, "sVV")
        sWL = [Sem(es, nc, "sWL0"), Sem(es, nc, "sWL1")]
        sWB = [Sem(es, nc, "sWB0"), Sem(es, nc, "sWB1")]
        sST = [Sem(es, nc, "sST0"), Sem(es, nc, "sST1")]
        sSS = Sem(es, nc, "sSS")
        ALLSEMS = ([sLD, sGD, sCC, sT, sS, sV] + sX + sY + sXC + sWs + sR
                   + sLA + [sLB, sVV] + sWL + sWB + sST + [sSS])

        def run_block(bodies, blk=None):
            snap = [s.n for s in ALLSEMS]
            DRY[0] = True
            dummy = _Dummy()
            for _, fn in bodies:
                fn(dummy)
            DRY[0] = False
            finals = [s.n for s in ALLSEMS]
            for s, v in zip(ALLSEMS, snap):
                s.n = v
            for name, fn in bodies:
                getattr(blk, name)(fn)
            for s, f in zip(ALLSEMS, finals):
                assert s.n == f, f"sem plan mismatch: {s.n} != {f}"

        _es1 = ExitStack()
        with _es1:
            e1 = _es1.enter_context

            def sb1(name, shape, dt=F32):
                return e1(nc.sbuf_tensor(name, shape, dt, side="right"))

            xht = sb1("xht", [128, 2, TQ])
            xhb = sb1("xhb", [128, 2, TQ], BF16)
            xhsq = sb1("xhsq", [128, 2, TQ], BF16)
            xpwTf = sb1("xpwTf", [128, NDT, 160])
            dtwTf = sb1("dtwTf", [128, NDT, 128])
            xp0f = sb1("xp0f", [128, L])
            xpBCr = sb1("xpBCr", [32, L])
            sumS = sb1("sumS", [1, L])
            sqS = sb1("sqS", [1, L])
            murow = sb1("murow", [1, L])
            rstdrow = sb1("rstdrow", [1, L])
            tmprow = sb1("tmprow", [1, L])
            nxpws = sb1("nxpws", [128, 1])
            nxpwsBC = sb1("nxpwsBC", [32, 1])
            xpwTb = sb1("xpwTb", [128, NDT, 160], BF16)
            ps_sum = e1(nc.psum_tensor("ps_sum", [1, TQ], F32))
            ps_sq = e1(nc.psum_tensor("ps_sq", [1, TQ], F32))
            ps_xp0 = e1(nc.psum_tensor("ps_xp0", [128, TQ], F32))
            ps_xpBC = e1(nc.psum_tensor("ps_xpBC", [32, TQ], F32))
            ps_brd = e1(nc.psum_tensor("ps_brd", [128, 2, TQ], F32))
            blk1 = e1(nc.Block())
            M = {}
            ps_ws = ps_brd[:, 0, 0:1]
            ps_wsBC = ps_brd[0:32, 1, 0:1]

            def b1_sync(sync):
                for src, dst in [
                    (xpwT.rearrange("(a p) r -> p a r", p=128), xpwTf[:]),
                    (dtwT.rearrange("r (a d) -> r a d", d=128), dtwTf[:]),
                    (Aarr_i, An[:]),
                    (Darr_i, Darr[:]),
                    (dtbarr_i, dtbarr[:]),
                    (selBC_i, selBCt[:]),
                    (onesrow_i, onesrowt[:]),
                    (onescol_b_i, onescolbt[:]),
                ]:
                    sLD.inc(sync.dma_start(out=dst, in_=src), 16)
                M["ld_const"] = sLD.n
                for it in range(NQ * NDT):
                    q, dt_i = it // NDT, it % NDT
                    if it >= 2:
                        W(sync, sV, M, ("cast", it - 2))
                    sX[it % 2].inc(sync.dma_start(
                        out=xht[:, it % 2, :],
                        in_=xTd[dt_i * 128:(dt_i + 1) * 128,
                                q * TQ:(q + 1) * TQ]), 16)
                    M[("ldx", it)] = sX[it % 2].n

            def b1_vec(v):
                def vsyncb(inst):
                    val = sVV.inc(inst)
                    if not DRY[0]:
                        v.wait_ge(sVV.h, val)

                def casts(q):
                    for dt_i in range(NDT):
                        it = q * NDT + dt_i
                        W(v, sX[it % 2], M, ("ldx", it))
                        if it >= 2:
                            W(v, sT, M, ("tmm", it - 2))
                        inst = v.tensor_copy(out=xhb[:, it % 2, :],
                                             in_=xht[:, it % 2, :])
                        vsyncb(inst)
                        inst = v.tensor_tensor(out=xhsq[:, it % 2, :],
                                               in0=xhb[:, it % 2, :],
                                               in1=xhb[:, it % 2, :],
                                               op=ALU.mult)
                        sV.inc(inst)
                        M[("cast", it)] = sV.n

                def finq(q):
                    sl = slice(q * TQ, (q + 1) * TQ)
                    W(v, sGD, M, ("gback", q))
                    v.tensor_scalar(out=murow[:, sl], in0=sumS[:, sl],
                                    scalar1=1.0 / D, scalar2=None,
                                    op0=ALU.mult)
                    inst = v.tensor_scalar(out=tmprow[:, sl], in0=sqS[:, sl],
                                           scalar1=1.0 / D, scalar2=None,
                                           op0=ALU.mult)
                    vsyncb(inst)
                    inst = v.tensor_tensor(out=rstdrow[:, sl],
                                           in0=murow[:, sl],
                                           in1=murow[:, sl], op=ALU.mult)
                    vsyncb(inst)
                    inst = v.tensor_tensor(out=tmprow[:, sl],
                                           in0=tmprow[:, sl],
                                           in1=rstdrow[:, sl],
                                           op=ALU.subtract)
                    sV.inc(inst)
                    M[("vfin1", q)] = sV.n
                    W(v, sS, M, ("sfin1", q))
                    inst = v.tensor_scalar(out=tmprow[:, sl],
                                           in0=tmprow[:, sl], scalar1=-0.5,
                                           scalar2=None, op0=ALU.mult)
                    sV.inc(inst)
                    M[("vfin2", q)] = sV.n
                    if q == 0:
                        W(v, sS, M, "sws")
                        v.tensor_scalar(out=nxpws[:], in0=nxpws[:],
                                        scalar1=-1.0, scalar2=None,
                                        op0=ALU.mult)
                        v.tensor_scalar(out=nxpwsBC[:], in0=nxpwsBC[:],
                                        scalar1=-1.0, scalar2=None,
                                        op0=ALU.mult)
                        W(v, sS, M, "sexpA")
                        inst = v.tensor_scalar(out=An[:], in0=An[:],
                                               scalar1=-1.0, scalar2=None,
                                               op0=ALU.mult)
                        vsyncb(inst)
                    W(v, sS, M, ("brd", q))
                    inst = v.scalar_tensor_tensor(out=xp0f[:, sl],
                                                  in0=mrep[:, sl],
                                                  scalar=nxpws[:],
                                                  in1=xp0f[:, sl],
                                                  op0=ALU.mult, op1=ALU.add)
                    vsyncb(inst)
                    inst = v.tensor_tensor(out=xp0f[:, sl], in0=xp0f[:, sl],
                                           in1=rrep[:, sl], op=ALU.mult)
                    vsyncb(inst)
                    inst = v.scalar_tensor_tensor(out=xpBCf[:, sl],
                                                  in0=mrep[0:32, sl],
                                                  scalar=nxpwsBC[:],
                                                  in1=xpBCr[:, sl],
                                                  op0=ALU.mult, op1=ALU.add)
                    vsyncb(inst)
                    inst = v.tensor_tensor(out=xpBCf[:, sl],
                                           in0=xpBCf[:, sl],
                                           in1=rrep[0:32, sl], op=ALU.mult)
                    vsyncb(inst)
                    inst = v.tensor_copy(out=xp0b[:, sl], in_=xp0f[:, sl])
                    sV.inc(inst)
                    M[("vxp", q)] = sV.n

                v.memset(epscol[:], EPS)
                v.memset(onescolf[:], 1.0)
                W(v, sLD, M, "ld_const")
                for i in range(NDT):
                    v.tensor_copy(out=xpwTb[:, i, :], in_=xpwTf[:, i, :])
                    inst = v.tensor_copy(out=dtwTb[:, i, :],
                                         in_=dtwTf[:, i, :])
                sV.inc(inst)
                M["vconst"] = sV.n
                casts(0)
                casts(1)
                finq(0)
                casts(2)
                finq(1)
                casts(3)
                finq(2)
                finq(3)

            def b1_te(te):
                def stream(q):
                    for dt_i in range(NDT):
                        it = q * NDT + dt_i
                        W(te, sV, M, ("cast", it))
                        if q >= 1 and dt_i == 0:
                            W(te, sS, M, ("evac", q - 1))
                        st, sp = dt_i == 0, dt_i == NDT - 1
                        te.matmul(out=ps_sum[:], lhsT=onescolbt[:],
                                  rhs=xhb[:, it % 2, :], start=st, stop=sp)
                        te.matmul(out=ps_sq[:], lhsT=onescolbt[:],
                                  rhs=xhsq[:, it % 2, :], start=st, stop=sp)
                        te.matmul(out=ps_xp0[:],
                                  lhsT=xpwTb[:, dt_i, 0:128],
                                  rhs=xhb[:, it % 2, :], start=st, stop=sp)
                        inst = te.matmul(out=ps_xpBC[:],
                                         lhsT=xpwTb[:, dt_i, 128:160],
                                         rhs=xhb[:, it % 2, :],
                                         start=st, stop=sp)
                        sT.inc(inst)
                        M[("tmm", it)] = sT.n

                def tbrd(q):
                    sl = slice(q * TQ, (q + 1) * TQ)
                    if q == 0:
                        W(te, sS, M, "sws")
                    else:
                        W(te, sS, M, ("brd", q - 1))
                    W(te, sS, M, ("fin", q))
                    te.matmul(out=ps_brd[:, 0, :], lhsT=onesrowt[:],
                              rhs=murow[:, sl], start=True, stop=True)
                    inst = te.matmul(out=ps_brd[:, 1, :], lhsT=onesrowt[:],
                                     rhs=rstdrow[:, sl], start=True, stop=True)
                    sT.inc(inst)
                    M[("tbrd", q)] = sT.n

                W(te, sV, M, "vconst")
                for i in range(NDT):
                    te.matmul(out=ps_ws, lhsT=xpwTb[:, i, 0:128],
                              rhs=onescolbt[:], start=i == 0,
                              stop=i == NDT - 1)
                    inst = te.matmul(out=ps_wsBC, lhsT=xpwTb[:, i, 128:160],
                                     rhs=onescolbt[:], start=i == 0,
                                     stop=i == NDT - 1)
                sT.inc(inst)
                M["tws"] = sT.n
                stream(0)
                stream(1)
                tbrd(0)
                stream(2)
                tbrd(1)
                stream(3)
                tbrd(2)
                tbrd(3)

            def b1_sc(sc):
                def evac(q):
                    sl = slice(q * TQ, (q + 1) * TQ)
                    W(sc, sT, M, ("tmm", q * NDT + NDT - 1))
                    sc.activation(out=sumS[:, sl], in_=ps_sum[:], func=AF.Copy)
                    sc.activation(out=sqS[:, sl], in_=ps_sq[:], func=AF.Copy)
                    sc.activation(out=xp0f[:, sl], in_=ps_xp0[:],
                                  func=AF.Copy)
                    inst = sc.activation(out=xpBCr[:, sl], in_=ps_xpBC[:],
                                         func=AF.Copy)
                    sS.inc(inst)
                    M[("evac", q)] = sS.n

                def sfin(q):
                    sl = slice(q * TQ, (q + 1) * TQ)
                    W(sc, sV, M, ("vfin1", q))
                    inst = sc.activation(out=tmprow[:, sl], in_=tmprow[:, sl],
                                         func=AF.Ln, bias=epscol[0:1, :])
                    sS.inc(inst)
                    M[("sfin1", q)] = sS.n
                    W(sc, sV, M, ("vfin2", q))
                    inst = sc.activation(out=rstdrow[:, sl],
                                         in_=tmprow[:, sl], func=AF.Exp)
                    sS.inc(inst)
                    M[("fin", q)] = sS.n

                def brde(q):
                    sl = slice(q * TQ, (q + 1) * TQ)
                    W(sc, sT, M, ("tbrd", q))
                    sc.activation(out=mrep[:, sl], in_=ps_brd[:, 0, :],
                                  func=AF.Copy)
                    inst = sc.activation(out=rrep[:, sl],
                                         in_=ps_brd[:, 1, :], func=AF.Copy)
                    sS.inc(inst)
                    M[("brd", q)] = sS.n

                W(sc, sT, M, "tws")
                sc.activation(out=nxpws[:], in_=ps_ws, func=AF.Copy)
                inst = sc.activation(out=nxpwsBC[:], in_=ps_wsBC,
                                     func=AF.Copy)
                sS.inc(inst)
                M["sws"] = sS.n
                W(sc, sLD, M, "ld_const")
                inst = sc.activation(out=An[:], in_=An[:], func=AF.Exp)
                sS.inc(inst)
                M["sexpA"] = sS.n
                evac(0)
                sfin(0)
                evac(1)
                brde(0)
                sfin(1)
                evac(2)
                brde(1)
                sfin(2)
                evac(3)
                brde(2)
                sfin(3)
                brde(3)

            def b1_g(g):
                for q in range(NQ):
                    sl = slice(q * TQ, (q + 1) * TQ)
                    W(g, sS, M, ("evac", q))
                    sGD.inc(g.dma_start(out=cc_sxq[q][0:128, :],
                                        in_=xp0f[:, sl]), 16)
                    sGD.inc(g.dma_start(out=cc_sxq[q][128:160, :],
                                        in_=xpBCr[:, sl]), 16)
                    sGD.inc(g.dma_start(out=cc_sxq[q][160:161, :],
                                        in_=sumS[:, sl]), 16)
                    sGD.inc(g.dma_start(out=cc_sxq[q][161:162, :],
                                        in_=sqS[:, sl]), 16)
                    M[("gout", q)] = sGD.n
                    W(g, sGD, M, ("gout", q))
                    sCC.inc(g.collective_compute(
                        "AllReduce", ALU.add,
                        replica_groups=[[0, 1], [2, 3], [4, 5], [6, 7]],
                        ins=[cc_sxq[q].opt()], outs=[cc_sxoq[q].opt()]))
                    M[("cc1", q)] = sCC.n
                    W(g, sCC, M, ("cc1", q))
                    sGD.inc(g.dma_start(out=xp0f[:, sl],
                                        in_=cc_sxoq[q][0:128, :]), 16)
                    sGD.inc(g.dma_start(out=xpBCr[:, sl],
                                        in_=cc_sxoq[q][128:160, :]), 16)
                    sGD.inc(g.dma_start(out=sumS[:, sl],
                                        in_=cc_sxoq[q][160:161, :]), 16)
                    sGD.inc(g.dma_start(out=sqS[:, sl],
                                        in_=cc_sxoq[q][161:162, :]), 16)
                    M[("gback", q)] = sGD.n
                    W(g, sGD, M, ("gback", q))

            run_block([("sync", b1_sync), ("vector", b1_vec),
                       ("tensor", b1_te), ("scalar", b1_sc),
                       ("gpsimd", b1_g)], blk1)

        _es2 = ExitStack()
        with _es2:
            BrepA = _es2.enter_context(nc.sbuf_tensor("BrepA", [128, NST, TQ], F32, side="right"))
            CrepA = _es2.enter_context(nc.sbuf_tensor("CrepA", [128, NST, TQ], F32, side="right"))
            dtb2 = _es2.enter_context(nc.sbuf_tensor("dtb2", [128, 2, TQ], F32, side="right"))
            dtxb = _es2.enter_context(nc.sbuf_tensor("dtxb", [128, TQ], F32, side="right"))
            xnb = _es2.enter_context(nc.sbuf_tensor("xnb", [128, TQ], F32, side="right"))
            dab = _es2.enter_context(nc.sbuf_tensor("dab", [128, 2, TQ], F32, side="right"))
            dbxh = _es2.enter_context(nc.sbuf_tensor("dbxh", [128, 4, TQ], F32, side="right"))
            hall = _es2.enter_context(nc.sbuf_tensor("hall", [128, NST, TQ], F32, side="right"))
            hcb = _es2.enter_context(nc.sbuf_tensor("hcb", [128, TQ], F32, side="right"))
            yac = _es2.enter_context(nc.sbuf_tensor("yac", [128, TQ], F32, side="right"))
            hbound = _es2.enter_context(nc.sbuf_tensor("hbound", [128, NDT * NST], F32, side="right"))
            xt2 = _es2.enter_context(nc.sbuf_tensor("xt2", [128, 2, TQ], F32, side="right"))
            x1t = _es2.enter_context(nc.sbuf_tensor("x1t", [128, 2, TQ], F32, side="right"))
            x1sq = _es2.enter_context(nc.sbuf_tensor("x1sq", [128, 2, TQ], BF16, side="right"))
            x1sumS = _es2.enter_context(nc.sbuf_tensor("x1sumS", [1, TQ], F32, side="right"))
            x1sqS = _es2.enter_context(nc.sbuf_tensor("x1sqS", [1, TQ], F32, side="right"))
            wtf = _es2.enter_context(nc.sbuf_tensor("wtf", [128, 2, 512], F32, side="right"))
            wtb = _es2.enter_context(nc.sbuf_tensor("wtb", [128, 2, 512], BF16, side="right"))
            sfu = _es2.enter_context(nc.sbuf_tensor("sfu", [128, TQ], F32, side="right"))
            sfl = _es2.enter_context(nc.sbuf_tensor("sfl", [128, TQ], F32, side="right"))
            sfr = _es2.enter_context(nc.sbuf_tensor("sfr", [128, TQ], F32, side="right"))
            ps_dt = _es2.enter_context(nc.psum_tensor("ps_dt", [128, 2, TQ], F32))
            ps_bc = _es2.enter_context(nc.psum_tensor("ps_bc", [128, 2, TQ], F32))
            p_x1s = _es2.enter_context(nc.psum_tensor("p_x1s", [1, TQ], F32))
            p_x1q = _es2.enter_context(nc.psum_tensor("p_x1q", [1, TQ], F32))
            blk2 = _es2.enter_context(nc.Block())
            M2 = {}

            def b2_sync(sync):
                for q in range(NQ):
                    for dt_i in range(NDT):
                        it = q * NDT + dt_i
                        if it >= 2:
                            W(sync, sV, M2, ("x1", (it - 2) // NDT,
                                             (it - 2) % NDT))
                        sX[it % 2].inc(sync.dma_start(
                            out=xt2[:, it % 2, :],
                            in_=xTd[dt_i * 128:(dt_i + 1) * 128,
                                    q * TQ:(q + 1) * TQ]), 16)
                        M2[("ldx", q, dt_i)] = sX[it % 2].n
                        for wj in range(2):
                            wi = it * 2 + wj
                            if wi >= 2:
                                W(sync, sS, M2, ("wc", wi - 2))
                            rw, cw = wi // 4, wi % 4
                            sWL[wi % 2].inc(sync.dma_start(
                                out=wtf[:, wi % 2, :],
                                in_=WTh[rw * 128:(rw + 1) * 128,
                                        cw * 512:(cw + 1) * 512]), 16)
                            M2[("wld", wi)] = sWL[wi % 2].n
                for src, dst in [
                    (biasFh, biasFt[:]),
                    (biota_i, biotat[:]),
                    (kiota_i, kiotat[:]),
                    (bsel16_i, bsel16t[:]),
                    (lsirep_i, lsit[:]),
                ]:
                    sLD.inc(sync.dma_start(out=dst, in_=src), 16)
                M2["lcst2"] = sLD.n
                W(sync, sCC, M2, ("ccq", 0))
                sST[0].inc(sync.dma_start(
                    out=stq[:], in_=cc_x1oq[0][DH:DH + 2, :]), 16)
                sST[0].inc(sync.dma_start(
                    out=stq2[:],
                    in_=cc_x1oq[0][DH + 2 + DH:2 * (DH + 2), :]), 16)
                M2["lst0"] = sST[0].n

            def b2_te(te):
                def x1stat_mm(pq, pdt):
                    it2 = pq * NDT + pdt
                    W(te, sV, M2, ("x1", pq, pdt))
                    W(te, sS, M2, ("x1q", pq, pdt))
                    if pdt == 0 and pq >= 1:
                        W(te, sS, M2, ("x1st", pq - 1))
                    te.matmul(out=p_x1s[:], lhsT=onescolf[:],
                              rhs=x1t[:, it2 % 2, :],
                              start=pdt == 0, stop=pdt == NDT - 1)
                    inst = te.matmul(out=p_x1q[:], lhsT=onescolbt[:],
                                     rhs=x1sq[:, it2 % 2, :],
                                     start=pdt == 0, stop=pdt == NDT - 1)
                    sT.inc(inst)
                    M2[("tx1", pq, pdt)] = sT.n

                def tdt_mm(q, dt_i):
                    it = q * NDT + dt_i
                    if it >= 2:
                        pit = it - 2
                        W(te, sS, M2, ("sfp", pit // NDT, pit % NDT))
                        W(te, sV, M2, ("dt", pit // NDT, pit % NDT))
                    inst = te.matmul(out=ps_dt[:, it % 2, :],
                                     lhsT=dtwTb[:, dt_i, :],
                                     rhs=xp0b[:, q * TQ:(q + 1) * TQ],
                                     start=True, stop=True)
                    sT.inc(inst)
                    M2[("tdt", q, dt_i)] = sT.n

                for q in range(NQ):
                    tdt_mm(q, 0)
                    tdt_mm(q, 1)
                    if q >= 1:
                        x1stat_mm(q - 1, NDT - 3)
                        x1stat_mm(q - 1, NDT - 2)
                    for j in range(32):
                        if q * 32 + j >= 2:
                            pj = q * 32 + j - 2
                            W(te, sS, M2, ("bc", pj // 32, pj % 32))
                        inst = te.matmul(out=ps_bc[:, j % 2, :],
                                         lhsT=selBCt[:, j * 128:(j + 1) * 128],
                                         rhs=xpBCf[:, q * TQ:(q + 1) * TQ],
                                         start=True, stop=True)
                        sT.inc(inst)
                        M2[("tbc", q, j)] = sT.n
                    if q >= 1:
                        x1stat_mm(q - 1, NDT - 1)
                    for dt_i in range(2, NDT):
                        tdt_mm(q, dt_i)
                        if dt_i >= 3:
                            x1stat_mm(q, dt_i - 3)
                for k in (NDT - 3, NDT - 2, NDT - 1):
                    x1stat_mm(NQ - 1, k)

            def b2_sc(sc):
                def bc_evac(q, j):
                    if q >= 1 and j == 0:
                        W(sc, sV, M2, ("scan", q - 1, NDT - 1, NST - 1))
                    if q >= 1 and j == 16:
                        W(sc, sV, M2, ("ylast", q - 1))
                    W(sc, sT, M2, ("tbc", q, j))
                    dst = BrepA if j < 16 else CrepA
                    inst = sc.activation(out=dst[:, j % 16, :],
                                         in_=ps_bc[:, j % 2, :],
                                         func=AF.Copy)
                    sS.inc(inst)
                    M2[("bc", q, j)] = sS.n

                def sq_cast(it2):
                    q2, d2 = it2 // NDT, it2 % NDT
                    W(sc, sV, M2, ("x1", q2, d2))
                    if it2 >= 2:
                        pit = it2 - 2
                        W(sc, sT, M2, ("tx1", pit // NDT, pit % NDT))
                    inst = sc.activation(out=x1sq[:, it2 % 2, :],
                                         in_=x1t[:, it2 % 2, :],
                                         func=AF.Square)
                    sS.inc(inst)
                    M2[("x1q", q2, d2)] = sS.n

                def x1st_evacs(q2):
                    W(sc, sT, M2, ("tx1", q2, NDT - 1))
                    if q2 >= 1:
                        W(sc, sGD, M2, ("gx1s", q2 - 1))
                    sc.activation(out=x1sumS[:], in_=p_x1s[:], func=AF.Copy)
                    inst = sc.activation(out=x1sqS[:], in_=p_x1q[:],
                                         func=AF.Copy)
                    sS.inc(inst)
                    M2[("x1st", q2)] = sS.n

                for q in range(NQ):
                    for dt_i in range(NDT):
                        it = q * NDT + dt_i
                        W(sc, sT, M2, ("tdt", q, dt_i))
                        if it >= 1:
                            W(sc, sV, M2, ("dt", (it - 1) // NDT,
                                           (it - 1) % NDT))
                        sc.activation(out=sfr[:], in_=ps_dt[:, it % 2, :],
                                      func=AF.Relu,
                                      bias=dtbarr[:, dt_i:dt_i + 1])
                        inst = sc.activation(out=sfu[:],
                                             in_=ps_dt[:, it % 2, :],
                                             func=AF.Abs,
                                             bias=dtbarr[:, dt_i:dt_i + 1])
                        val = sSS.inc(inst)
                        if not DRY[0]:
                            sc.wait_ge(sSS.h, val)
                        inst = sc.activation(out=sfu[:], in_=sfu[:],
                                             func=AF.Exp, scale=-1.0)
                        val = sSS.inc(inst)
                        if not DRY[0]:
                            sc.wait_ge(sSS.h, val)
                        inst = sc.activation(out=sfl[:], in_=sfu[:],
                                             func=AF.Ln, bias=1.0)
                        sS.inc(inst)
                        M2[("sfp", q, dt_i)] = sS.n
                        for n in range(NST):
                            if dt_i == 0 and n % 4 == 0:
                                for j in range(n, n + 4):
                                    bc_evac(q, j)
                            if n == 0:
                                W(sc, sV, M2, ("dt", q, dt_i))
                            if n >= 2:
                                W(sc, sV, M2, ("scan", q, dt_i, n - 2))
                            elif it >= 1:
                                pit = it - 1
                                W(sc, sV, M2, ("scan", pit // NDT, pit % NDT,
                                               NST - 2 + n))
                            inst = sc.activation(
                                out=dab[:, n % 2, :], in_=dtb2[:, it % 2, :],
                                func=AF.Exp,
                                scale=An[:, dt_i * NST + n:dt_i * NST + n + 1])
                            sS.inc(inst)
                            M2[("exp", q, dt_i, n)] = sS.n
                        if dt_i == 0:
                            for j in range(16, 32):
                                bc_evac(q, j)
                        for wj in range(2):
                            wi = it * 2 + wj
                            W(sc, sWL[wi % 2], M2, ("wld", wi))
                            if wi >= 2:
                                W(sc, sWB[wi % 2], M2, ("wst", wi - 2))
                            inst = sc.activation(out=wtb[:, wi % 2, :],
                                                 in_=wtf[:, wi % 2, :],
                                                 func=AF.Copy)
                            sS.inc(inst)
                            M2[("wc", wi)] = sS.n
                        if it >= 1:
                            sq_cast(it - 1)
                        if dt_i == 0 and q >= 1:
                            x1st_evacs(q - 1)
                sq_cast(NQ * NDT - 1)
                x1st_evacs(NQ - 1)

            def b2_v(v):
                def vsync(inst):
                    val = sVV.inc(inst)
                    if not DRY[0]:
                        v.wait_ge(sVV.h, val)

                inst = v.memset(hbound[:], 0.0)
                vsync(inst)
                for q in range(NQ):
                    qsl = slice(q * TQ, (q + 1) * TQ)
                    for dt_i in range(NDT):
                        it = q * NDT + dt_i
                        W(v, sS, M2, ("sfp", q, dt_i))
                        if it >= 2:
                            pit = it - 2
                            W(v, sS, M2, ("exp", pit // NDT, pit % NDT,
                                          NST - 1))
                        inst = v.tensor_tensor(out=dtb2[:, it % 2, :],
                                               in0=sfr[:], in1=sfl[:],
                                               op=ALU.add)
                        sV.inc(inst)
                        M2[("dt", q, dt_i)] = sV.n
                        W(v, sX[it % 2], M2, ("ldx", q, dt_i))
                        inst = v.tensor_tensor(out=xnb[:], in0=xt2[:, it % 2, :],
                                               in1=mrep[:, qsl], op=ALU.subtract)
                        vsync(inst)
                        inst = v.tensor_tensor(out=xnb[:], in0=xnb[:],
                                               in1=rrep[:, qsl], op=ALU.mult)
                        vsync(inst)
                        inst = v.tensor_tensor(out=dtxb[:],
                                               in0=dtb2[:, it % 2, :],
                                               in1=xnb[:], op=ALU.mult)
                        vsync(inst)
                        dtxb4 = dtxb[:].rearrange(
                            "p (o f) -> p o f", o=1).broadcast_to((128, 4, TQ))
                        for grp in range(4):
                            if dt_i == 0:
                                W(v, sS, M2, ("bc", q, grp * 4 + 3))
                            inst = v.tensor_tensor(
                                out=dbxh[:],
                                in0=dtxb4,
                                in1=BrepA[:, grp * 4:(grp + 1) * 4, :],
                                op=ALU.mult)
                            vsync(inst)
                            for nn in range(4):
                                n = grp * 4 + nn
                                W(v, sS, M2, ("exp", q, dt_i, n))
                                slot = dt_i * NST + n
                                inst = v.tensor_tensor_scan(
                                    out=hall[:, n, :],
                                    data0=dab[:, n % 2, :],
                                    data1=dbxh[:, nn, :],
                                    initial=hbound[:, slot:slot + 1],
                                    op0=ALU.mult, op1=ALU.add)
                                sV.inc(inst)
                                M2[("scan", q, dt_i, n)] = sV.n
                            if not DRY[0]:
                                v.wait_ge(sV.h, M2[("scan", q, dt_i,
                                                    grp * 4 + 3)])
                        inst = v.tensor_copy(
                            out=hbound[:, dt_i * NST:(dt_i + 1) * NST],
                            in_=hall[:, :, TQ - 1:TQ].rearrange(
                                "p n o -> p (n o)"))
                        vsync(inst)
                        if dt_i == 0:
                            W(v, sS, M2, ("bc", q, 31))
                        inst = v.tensor_tensor(out=hall[:], in0=hall[:],
                                               in1=CrepA[:], op=ALU.mult)
                        vsync(inst)
                        for lvl in (8, 4, 2):
                            inst = v.tensor_tensor(
                                out=hall[:, 0:lvl, :],
                                in0=hall[:, 0:lvl, :],
                                in1=hall[:, lvl:2 * lvl, :],
                                op=ALU.add)
                            vsync(inst)
                        inst = v.tensor_tensor(out=yac[:], in0=hall[:, 0, :],
                                               in1=hall[:, 1, :], op=ALU.add)
                        vsync(inst)
                        if it >= 2:
                            pit = it - 2
                            W(v, sY[pit % 2], M2, ("gx1", pit // NDT, pit % NDT))
                            W(v, sT, M2, ("tx1", pit // NDT, pit % NDT))
                        inst = v.scalar_tensor_tensor(out=hcb[:], in0=xnb[:],
                                                      scalar=Darr[:, dt_i:dt_i + 1],
                                                      in1=yac[:],
                                                      op0=ALU.mult, op1=ALU.add)
                        vsync(inst)
                        inst = v.tensor_tensor(out=x1t[:, it % 2, :],
                                               in0=hcb[:],
                                               in1=xt2[:, it % 2, :],
                                               op=ALU.add)
                        val = sV.inc(inst)
                        M2[("x1", q, dt_i)] = val
                        if not DRY[0]:
                            v.wait_ge(sV.h, val)
                    inst = v.tensor_copy(out=hcb[:, 0:1], in_=hcb[:, 0:1])
                    sV.inc(inst)
                    M2[("ylast", q)] = sV.n

            def b2_g(g):
                for q in range(NQ):
                    for dt_i in range(NDT):
                        it = q * NDT + dt_i
                        W(g, sV, M2, ("x1", q, dt_i))
                        sY[it % 2].inc(g.dma_start(
                            out=cc_x1q[q][dt_i * 128:(dt_i + 1) * 128, :],
                            in_=x1t[:, it % 2, :]), 16)
                        M2[("gx1", q, dt_i)] = sY[it % 2].n
                        for wj in range(2):
                            wi = it * 2 + wj
                            rw, cw = wi // 4, wi % 4
                            W(g, sS, M2, ("wc", wi))
                            sWB[wi % 2].inc(g.dma_start(
                                out=wbf[rw * 128:(rw + 1) * 128,
                                        cw * 512:(cw + 1) * 512],
                                in_=wtb[:, wi % 2, :]), 16)
                            M2[("wst", wi)] = sWB[wi % 2].n
                    W(g, sS, M2, ("x1st", q))
                    sGD.inc(g.dma_start(out=cc_x1q[q][DH:DH + 1, :],
                                        in_=x1sumS[:]), 16)
                    sGD.inc(g.dma_start(out=cc_x1q[q][DH + 1:DH + 2, :],
                                        in_=x1sqS[:]), 16)
                    M2[("gx1s", q)] = sGD.n
                    M2[("yq0", q)] = sY[0].n
                    M2[("yq1", q)] = sY[1].n
                    W(g, sGD, M2, ("gx1s", q))
                    W(g, sY[0], M2, ("yq0", q))
                    W(g, sY[1], M2, ("yq1", q))
                    sCC.inc(g.collective_compute(
                        "AllGather", ALU.bypass,
                        replica_groups=[[0, 1], [2, 3], [4, 5], [6, 7]],
                        ins=[cc_x1q[q].opt()], outs=[cc_x1oq[q].opt()]))
                    M2[("ccq", q)] = sCC.n
                M2["cc2"] = sCC.n
                M2["wstot0"] = sWB[0].n
                M2["wstot1"] = sWB[1].n
                W(g, sWB[0], M2, "wstot0")
                W(g, sWB[1], M2, "wstot1")

            run_block([("sync", b2_sync), ("tensor", b2_te),
                       ("scalar", b2_sc), ("vector", b2_v),
                       ("gpsimd", b2_g)], blk2)

        es12.close()
        _es3 = ExitStack()
        with _es3:
            e3 = _es3.enter_context
            xn2b = e3(nc.sbuf_tensor("xn2b", [128, NCH, L], BF16))
            x1ch = e3(nc.sbuf_tensor("x1ch", [128, 6, TS], F32))
            vrq = e3(nc.sbuf_tensor("vrq", [1, TS], F32))
            rsq = e3(nc.sbuf_tensor("rsq", [1, TS], F32))
            mrep2 = e3(nc.sbuf_tensor("mrep2", [128, 2, TS], F32))
            rrep2 = e3(nc.sbuf_tensor("rrep2", [128, 2, TS], F32))
            wslb = e3(nc.sbuf_tensor("wslb", [128, 2, NCH, 128], BF16))
            latf1 = e3(nc.sbuf_tensor("latf1", [128, 4, 48], F32))
            lATt = e3(nc.sbuf_tensor("lATt", [128, NCH, 48], BF16))
            lBT1 = e3(nc.sbuf_tensor("lBT1", [16, DH], F32))
            l16buf = e3(nc.sbuf_tensor("l16buf", [16, DH], F32))
            lBeb = e3(nc.sbuf_tensor("lBeb", [16, DH], BF16))
            gt1 = e3(nc.sbuf_tensor("gt1", [4, 4], F32))
            gt2 = e3(nc.sbuf_tensor("gt2", [4, 1], F32))
            gt3 = e3(nc.sbuf_tensor("gt3", [4, 1], F32))
            gt4 = e3(nc.sbuf_tensor("gt4", [4, 3], F32))
            wk16 = e3(nc.sbuf_tensor("wk16", [16, 3], F32))
            rselb = e3(nc.sbuf_tensor("rselb", [16, L], BF16))
            x1r = e3(nc.sbuf_tensor("x1r", [128, 2, TS], F32))
            x2t = e3(nc.sbuf_tensor("x2t", [128, TS], F32))
            p_br2 = e3(nc.psum_tensor("p_br2", [128, 2, TS], F32))
            p_rk = e3(nc.psum_tensor("p_rk", [16, TS], F32))
            p_wk = e3(nc.psum_tensor("p_wk", [16, 3], F32))
            p_mm = e3(nc.psum_tensor("p_mm", [128, 2, TS], F32))
            blk3 = e3(nc.Block())
            M3 = {}

            def orow(q, rank):
                return rank * (DH + 2)

            lBef = l16buf[:, 0:DH]

            def rkt(k):
                return l16buf[:, k * TS:(k + 1) * TS]

            rself = l16buf[:, 3 * TS:4 * TS]

            def b3_g(g):
                def gvr_fetch(ts_i):
                    W(g, sV, M3, ("vf1", ts_i))
                    sGD.inc(g.dma_start(out=vrq[:], in_=stq[1:2, :]), 16)
                    M3[("gvr", ts_i)] = sGD.n

                gvr_fetch(0)
                for ot in range(2):
                    for c4 in range(4):
                        rs = c4 * 1024
                        inst = g.dma_start(
                            out=wslb[:, ot % 2, c4 * 8:(c4 + 1) * 8, :],
                            in_=wbf[rs:rs + 1024,
                                    ot * 128:(ot + 1) * 128].rearrange(
                                        "(c p) o -> p c o", p=128))
                        sWs[ot % 2].inc(inst, 16)
                    M3[("lw", ot)] = sWs[ot % 2].n
                for ts_i in range(1, NTS):
                    gvr_fetch(ts_i)
                for ot in range(NOT):
                    for ts_i in range(NTS):
                        W(g, sV, M3, ("x2", ot, ts_i))
                        sGD.inc(g.dma_start(
                            out=oq[ot * 128:(ot + 1) * 128,
                                   ts_i * TS:(ts_i + 1) * TS],
                            in_=x2t[:]), 16)
                        M3[("go", ot, ts_i)] = sGD.n
                    if ot + 2 < NOT:
                        po = ot + 2
                        W(g, sT, M3, ("tmm", ot, NTS - 1))
                        for c4 in range(4):
                            rs = c4 * 1024
                            inst = g.dma_start(
                                out=wslb[:, po % 2, c4 * 8:(c4 + 1) * 8, :],
                                in_=wbf[rs:rs + 1024,
                                        po * 128:(po + 1) * 128].rearrange(
                                            "(c p) o -> p c o", p=128))
                            sWs[po % 2].inc(inst, 16)
                        M3[("lw", po)] = sWs[po % 2].n
                W(g, sGD, M3, ("go", NOT - 1, NTS - 1))

            def b3_sync(sync):
                def stats_load(ts_i):
                    if ts_i >= 1:
                        W(sync, sT, M3, ("tbr2", ts_i - 1))
                    W(sync, sCC, M2, ("ccq", ts_i))
                    r0 = orow(ts_i, 0)
                    r1 = orow(ts_i, 1)
                    sST[ts_i % 2].inc(sync.dma_start(
                        out=stq[:],
                        in_=cc_x1oq[ts_i][r0 + DH:r0 + DH + 2, :]), 16)
                    sST[ts_i % 2].inc(sync.dma_start(
                        out=stq2[:],
                        in_=cc_x1oq[ts_i][r1 + DH:r1 + DH + 2, :]), 16)
                    M3[("lst", ts_i)] = sST[ts_i % 2].n

                def x1_loads(ts_i, cs=0, ce=NCH):
                    r0 = orow(ts_i, 0)
                    r1 = orow(ts_i, 1)
                    for c in range(cs, ce):
                        k = ts_i * NCH + c
                        if k >= 6:
                            pk = k - 6
                            W(sync, sV, M3, ("xn2", pk // NCH, pk % NCH))
                        src_row = (r0 + c * 128 if c < NDT
                                   else r1 + (c - NDT) * 128)
                        sXC[k % 6].inc(sync.dma_start(
                            out=x1ch[:, k % 6, :],
                            in_=cc_x1oq[ts_i][src_row:src_row + 128, :]), 16)
                        M3[("lx1", ts_i, c)] = sXC[k % 6].n

                def lbt_load(k):
                    if k >= 1:
                        W(sync, sV, M3, ("vlb", k - 1))
                    sLB.inc(sync.dma_start(
                        out=lBT1[:], in_=lBTh_i[:, k * DH:(k + 1) * DH]), 16)
                    M3[("llb", k)] = sLB.n

                x1_loads(0, 0, 8)
                lbt_load(0)
                x1_loads(0, 8, 20)
                stats_load(1)
                x1_loads(0, 20, NCH)
                for c in range(NCH):
                    if c >= 4:
                        W(sync, sS, M3, ("lat", c - 4))
                    sLA[c % 4].inc(sync.dma_start(
                        out=latf1[:, c % 4, :],
                        in_=lATf_i[c * 128:(c + 1) * 128, :]), 16)
                    M3[("llat", c)] = sLA[c % 4].n
                lbt_load(1)
                lbt_load(2)
                stats_load(2)
                x1_loads(1)
                stats_load(3)
                x1_loads(2)
                x1_loads(3)
                for ot in range(NOT):
                    for ts_i in range(NTS):
                        k = ot * NTS + ts_i
                        if k >= 2:
                            pk = k - 2
                            W(sync, sV, M3, ("x2", pk // NTS, pk % NTS))
                        sR[k % 2].inc(sync.dma_start(
                            out=x1r[:, k % 2, :],
                            in_=cc_x1q[ts_i][ot * 128:(ot + 1) * 128,
                                             :]), 16)
                        M3[("lx1r", ot, ts_i)] = sR[k % 2].n

            def b3_te1(te):
                def trk_mms(ts_i):
                    sl = slice(ts_i * TS, (ts_i + 1) * TS)
                    if ts_i == 0:
                        W(te, sS, M3, ("lat", NCH - 1))
                    W(te, sV, M3, ("xn2", ts_i, NCH - 1))
                    for k in range(3):
                        if ts_i * 3 + k >= 1:
                            pk = ts_i * 3 + k - 1
                            W(te, sS, M3, ("rk", pk // 3, pk % 3))
                        for c in range(NCH):
                            inst = te.matmul(
                                out=p_rk[:],
                                lhsT=lATt[:, c, k * 16:(k + 1) * 16],
                                rhs=xn2b[:, c, sl],
                                start=c == 0, stop=c == NCH - 1)
                        sT.inc(inst)
                        M3[("trk", ts_i, k)] = sT.n

                def tbr2_mms(ts_i):
                    if ts_i >= 1:
                        W(te, sS, M3, ("br2", ts_i - 1))
                    W(te, sS, M3, ("srs", ts_i))
                    te.matmul(out=p_br2[:, 0, :], lhsT=onesrowt[:],
                              rhs=stq[0:1, :], start=True, stop=True)
                    inst = te.matmul(out=p_br2[:, 1, :], lhsT=onesrowt[:],
                                     rhs=rsq[:], start=True, stop=True)
                    sT.inc(inst)
                    M3[("tbr2", ts_i)] = sT.n

                tbr2_mms(0)
                W(te, sV, M3, "vgate")
                inst = te.matmul(out=p_wk[:], lhsT=bsel16t[:],
                                 rhs=gt4[:], start=True, stop=True)
                sT.inc(inst)
                M3["twk"] = sT.n
                tbr2_mms(1)
                trk_mms(0)
                tbr2_mms(2)
                trk_mms(1)
                tbr2_mms(3)
                trk_mms(2)
                trk_mms(3)

            def b3_s3(sc):
                def chain_s(ts_i):
                    W(sc, sV, M3, ("vvar", ts_i))
                    inst = sc.activation(out=vrq[:], in_=vrq[:], func=AF.Ln,
                                         bias=epscol[0:1, :])
                    sS.inc(inst)
                    M3[("sln", ts_i)] = sS.n
                    W(sc, sV, M3, ("vnh", ts_i))
                    inst = sc.activation(out=rsq[:], in_=vrq[:], func=AF.Exp)
                    sS.inc(inst)
                    M3[("srs", ts_i)] = sS.n
                    W(sc, sT, M3, ("tbr2", ts_i))
                    if ts_i >= 2:
                        W(sc, sV, M3, ("xn2", ts_i - 2, NCH - 1))
                    sc.activation(out=mrep2[:, ts_i % 2, :],
                                  in_=p_br2[:, 0, :], func=AF.Copy)
                    inst = sc.activation(out=rrep2[:, ts_i % 2, :],
                                         in_=p_br2[:, 1, :], func=AF.Copy)
                    sS.inc(inst)
                    M3[("br2", ts_i)] = sS.n

                def rk_evacs(ts_i):
                    for k in range(3):
                        W(sc, sT, M3, ("trk", ts_i, k))
                        if ts_i == 0 and k == 0:
                            W(sc, sV, M3, "vlbe")
                        if ts_i >= 1:
                            W(sc, sV, M3, ("rsel", ts_i - 1))
                        inst = sc.activation(out=rkt(k), in_=p_rk[:],
                                             func=AF.Copy)
                        sS.inc(inst)
                        M3[("rk", ts_i, k)] = sS.n

                chain_s(0)
                W(sc, sT, M3, "twk")
                inst = sc.activation(out=wk16[:], in_=p_wk[:], func=AF.Copy)
                sS.inc(inst)
                M3["swk"] = sS.n
                chain_s(1)
                for c in range(NCH):
                    W(sc, sLA[c % 4], M3, ("llat", c))
                    inst = sc.activation(out=lATt[:, c, :],
                                         in_=latf1[:, c % 4, :],
                                         func=AF.Copy)
                    sS.inc(inst)
                    M3[("lat", c)] = sS.n
                rk_evacs(0)
                chain_s(2)
                rk_evacs(1)
                chain_s(3)
                rk_evacs(2)
                rk_evacs(3)

            def b3_v3(v):
                def vsync3(inst):
                    val = sVV.inc(inst)
                    if not DRY[0]:
                        v.wait_ge(sVV.h, val)

                def gate_p1():
                    W(v, sLD, M2, "lcst2")
                    inst = v.tensor_copy(out=lsif[:], in_=lsit[:])
                    vsync3(inst)
                    inst = v.tensor_tensor(out=gt1[:], in0=biotat[:],
                                           in1=lsif[:], op=ALU.is_lt)
                    vsync3(inst)
                    inst = v.tensor_scalar(out=gt2[:], in0=gt1[:, 0:1],
                                           scalar1=-1.0, scalar2=1.0,
                                           op0=ALU.mult, op1=ALU.add)
                    vsync3(inst)
                    inst = v.tensor_tensor(out=gt2[:], in0=gt2[:],
                                           in1=gt1[:, 1:2], op=ALU.mult)
                    vsync3(inst)
                    inst = v.tensor_scalar(out=gt3[:], in0=gt1[:, 1:2],
                                           scalar1=-2.0, scalar2=2.0,
                                           op0=ALU.mult, op1=ALU.add)
                    vsync3(inst)

                def gate_p2():
                    inst = v.tensor_tensor(out=gt2[:], in0=gt2[:], in1=gt3[:],
                                           op=ALU.add)
                    vsync3(inst)
                    inst = v.tensor_scalar(out=gt3[:], in0=gt1[:, 3:4],
                                           scalar1=-1.0, scalar2=4.0,
                                           op0=ALU.mult, op1=ALU.add)
                    vsync3(inst)
                    inst = v.tensor_tensor(out=gt2[:], in0=gt2[:], in1=gt3[:],
                                           op=ALU.subtract)
                    vsync3(inst)
                    inst = v.tensor_tensor(out=gt2[:], in0=gt2[:],
                                           in1=gt1[:, 2:3], op=ALU.mult)
                    vsync3(inst)
                    inst = v.tensor_tensor(out=gt2[:], in0=gt2[:], in1=gt3[:],
                                           op=ALU.add)
                    vsync3(inst)

                def gate_p3():
                    inst = v.tensor_scalar(out=gt3[:], in0=gt2[:],
                                           scalar1=-2.0, scalar2=0.0,
                                           op0=ALU.add, op1=ALU.max)
                    vsync3(inst)
                    inst = v.tensor_scalar(out=gt3[:], in0=gt3[:], scalar1=2.0,
                                           scalar2=None, op0=ALU.min)
                    vsync3(inst)
                    inst = v.tensor_scalar(out=gt2[:], in0=gt2[:], scalar1=2.0,
                                           scalar2=None, op0=ALU.is_ge)
                    vsync3(inst)
                    inst = v.tensor_scalar(out=gt4[:], in0=kiotat[:],
                                           scalar1=gt3[:], scalar2=None,
                                           op0=ALU.is_equal)
                    vsync3(inst)
                    inst = v.tensor_scalar(out=gt4[:], in0=gt4[:],
                                           scalar1=gt2[:], scalar2=None,
                                           op0=ALU.mult)
                    sV.inc(inst)
                    M3["vgate"] = sV.n

                def chain_a(ts_i):
                    if ts_i >= 1:
                        W(v, sT, M3, ("tbr2", ts_i - 1))
                        W(v, sST[ts_i % 2], M3, ("lst", ts_i))
                    else:
                        W(v, sST[0], M2, "lst0")
                    inst = v.tensor_tensor(out=stq[:], in0=stq[:],
                                           in1=stq2[:], op=ALU.add)
                    vsync3(inst)
                    inst = v.tensor_scalar(out=stq[:], in0=stq[:],
                                           scalar1=1.0 / D, scalar2=None,
                                           op0=ALU.mult)
                    sV.inc(inst)
                    M3[("vf1", ts_i)] = sV.n

                def chain_b(ts_i):
                    W(v, sGD, M3, ("gvr", ts_i))
                    inst = v.tensor_tensor(out=stq2[0:1, :], in0=stq[0:1, :],
                                           in1=stq[0:1, :], op=ALU.mult)
                    vsync3(inst)
                    inst = v.tensor_tensor(out=vrq[:], in0=vrq[:],
                                           in1=stq2[0:1, :], op=ALU.subtract)
                    sV.inc(inst)
                    M3[("vvar", ts_i)] = sV.n

                def chain_c(ts_i):
                    W(v, sS, M3, ("sln", ts_i))
                    inst = v.tensor_scalar(out=vrq[:], in0=vrq[:],
                                           scalar1=-0.5, scalar2=None,
                                           op0=ALU.mult)
                    sV.inc(inst)
                    M3[("vnh", ts_i)] = sV.n

                def lbeff(k):
                    if k == 0:
                        W(v, sS, M3, "swk")
                    W(v, sLB, M3, ("llb", k))
                    if k == 0:
                        inst = v.tensor_scalar(out=lBef, in0=lBT1[:],
                                               scalar1=wk16[:, 0:1],
                                               scalar2=None, op0=ALU.mult)
                    else:
                        inst = v.scalar_tensor_tensor(out=lBef, in0=lBT1[:],
                                                      scalar=wk16[:, k:k + 1],
                                                      in1=lBef,
                                                      op0=ALU.mult,
                                                      op1=ALU.add)
                    vsync3(inst)
                    sV.inc(v.tensor_copy(out=gt1[:, 0:1], in_=gt1[:, 0:1]))
                    M3[("vlb", k)] = sV.n

                def lbeb():
                    inst = v.tensor_copy(out=lBeb[:], in_=lBef)
                    sV.inc(inst)
                    M3["vlbe"] = sV.n

                def rsel(ts_i):
                    sl = slice(ts_i * TS, (ts_i + 1) * TS)
                    W(v, sS, M3, ("rk", ts_i, 2))
                    inst = v.tensor_scalar(out=rself, in0=rkt(0),
                                           scalar1=wk16[:, 0:1], scalar2=None,
                                           op0=ALU.mult)
                    for k in (1, 2):
                        vsync3(inst)
                        inst = v.scalar_tensor_tensor(out=rself, in0=rkt(k),
                                                      scalar=wk16[:, k:k + 1],
                                                      in1=rself,
                                                      op0=ALU.mult,
                                                      op1=ALU.add)
                    vsync3(inst)
                    inst = v.tensor_copy(out=rselb[:, sl], in_=rself)
                    sV.inc(inst)
                    M3[("rsel", ts_i)] = sV.n

                gate_p1()
                chain_a(0)
                gate_p2()
                chain_b(0)
                gate_p3()
                chain_c(0)

                fillers = {}

                def add_f(ts_i, c, fn):
                    fillers.setdefault((ts_i, c), []).append(fn)

                add_f(0, 20, lambda: chain_a(1))
                add_f(0, 23, lambda: chain_b(1))
                add_f(0, 26, lambda: chain_c(1))
                add_f(0, 27, lambda: lbeff(0))
                add_f(0, 28, lambda: lbeff(1))
                add_f(0, 29, lambda: lbeff(2))
                add_f(0, 30, lbeb)
                for ts_i in (1, 2):
                    add_f(ts_i, 10, (lambda t: lambda: rsel(t))(ts_i - 1))
                    add_f(ts_i, 12, (lambda t: lambda: chain_a(t))(ts_i + 1))
                    add_f(ts_i, 18, (lambda t: lambda: chain_b(t))(ts_i + 1))
                    add_f(ts_i, 24, (lambda t: lambda: chain_c(t))(ts_i + 1))
                add_f(3, 10, lambda: rsel(2))

                for ts_i in range(NTS):
                    sl = slice(ts_i * TS, (ts_i + 1) * TS)
                    for c in range(NCH):
                        k = ts_i * NCH + c
                        W(v, sXC[k % 6], M3, ("lx1", ts_i, c))
                        if c == 0:
                            W(v, sS, M3, ("br2", ts_i))
                        inst = v.tensor_tensor(out=x1ch[:, k % 6, :],
                                               in0=x1ch[:, k % 6, :],
                                               in1=mrep2[:, ts_i % 2, :],
                                               op=ALU.subtract)
                        vsync3(inst)
                        inst = v.tensor_tensor(out=xn2b[:, c, sl],
                                               in0=x1ch[:, k % 6, :],
                                               in1=rrep2[:, ts_i % 2, :],
                                               op=ALU.mult)
                        sV.inc(inst)
                        M3[("xn2", ts_i, c)] = sV.n
                        for f in fillers.get((ts_i, c), []):
                            f()
                rsel(3)

            def b3_te2(te):
                for ot in range(NOT):
                    W(te, sWs[ot % 2], M3, ("lw", ot))
                    for ts_i in range(NTS):
                        k = ot * NTS + ts_i
                        sl = slice(ts_i * TS, (ts_i + 1) * TS)
                        if ot == 0:
                            W(te, sV, M3, ("xn2", ts_i, NCH - 1))
                            W(te, sV, M3, ("rsel", ts_i))
                        if k >= 2:
                            pk = k - 2
                            W(te, sS, M3, ("x2s", pk // NTS, pk % NTS))
                        for c in range(NCH):
                            te.matmul(out=p_mm[:, k % 2, :],
                                      lhsT=wslb[:, ot % 2, c, :],
                                      rhs=xn2b[:, c, sl],
                                      start=c == 0, stop=False)
                        inst = te.matmul(out=p_mm[:, k % 2, :],
                                         lhsT=lBeb[:, ot * 128:(ot + 1) * 128],
                                         rhs=rselb[:, sl],
                                         start=False, stop=True)
                        sT.inc(inst)
                        M3[("tmm", ot, ts_i)] = sT.n

            def b3_s4(sc):
                for ot in range(NOT):
                    for ts_i in range(NTS):
                        k = ot * NTS + ts_i
                        W(sc, sT, M3, ("tmm", ot, ts_i))
                        if k >= 1:
                            pk = k - 1
                            W(sc, sGD, M3, ("go", pk // NTS, pk % NTS))
                        inst = sc.activation(out=x2t[:],
                                             in_=p_mm[:, k % 2, :],
                                             func=AF.Identity,
                                             bias=biasFt[:, ot:ot + 1])
                        sS.inc(inst)
                        M3[("x2s", ot, ts_i)] = sS.n

            def b3_v4(v):
                for ot in range(NOT):
                    for ts_i in range(NTS):
                        k = ot * NTS + ts_i
                        W(v, sS, M3, ("x2s", ot, ts_i))
                        W(v, sR[k % 2], M3, ("lx1r", ot, ts_i))
                        if k >= 2:
                            pk = k - 2
                            W(v, sGD, M3, ("go", pk // NTS, pk % NTS))
                        inst = v.tensor_tensor(out=x2t[:],
                                               in0=x2t[:],
                                               in1=x1r[:, k % 2, :],
                                               op=ALU.add)
                        sV.inc(inst)
                        M3[("x2", ot, ts_i)] = sV.n

            run_block([("gpsimd", b3_g), ("sync", b3_sync),
                       ("tensor", b3_te1), ("scalar", b3_s3),
                       ("vector", b3_v3), ("tensor", b3_te2),
                       ("scalar", b3_s4), ("vector", b3_v4)], blk3)
            import os as _os
            if _os.environ.get("DUMP_M3"):
                import json as _json
                _json.dump({str(k): v for k, v in M3.items()},
                           open("/tmp/m3.json", "w"))
                _json.dump({str(k): v for k, v in M2.items()},
                           open("/tmp/m2.json", "w"))

    return nc


_NC_CACHE = None


def kernel(L0_attn_x, L0_attn_A_log, L0_attn_D, L0_attn_x_proj_w,
           L0_attn_dt_proj_w, L0_attn_dt_proj_b, L0_ffn_weight, L0_ffn_bias,
           L0_ffn_lora_A, L0_ffn_lora_B, L0_ffn_lora_start_indices):
    global _NC_CACHE
    import ml_dtypes
    x = np.ascontiguousarray(np.asarray(L0_attn_x, dtype=np.float32))
    A_log = np.asarray(L0_attn_A_log, dtype=np.float32)
    Dv = np.asarray(L0_attn_D, dtype=np.float32)
    xpw = np.asarray(L0_attn_x_proj_w, dtype=np.float32)
    dtw = np.asarray(L0_attn_dt_proj_w, dtype=np.float32)
    dtb = np.asarray(L0_attn_dt_proj_b, dtype=np.float32)
    Wm = np.asarray(L0_ffn_weight, dtype=np.float32)
    biasF = np.asarray(L0_ffn_bias, dtype=np.float32)
    lA = np.asarray(L0_ffn_lora_A, dtype=np.float32)
    lB = np.asarray(L0_ffn_lora_B, dtype=np.float32)
    lsi = np.asarray(L0_ffn_lora_start_indices)

    WT = np.ascontiguousarray(Wm.T)
    xpwTm = np.ascontiguousarray(xpw.T)
    dtwTm = np.ascontiguousarray(dtw.T)
    lATf = np.ascontiguousarray(lA.transpose(2, 0, 1).reshape(D, 48))
    lBT = lB.transpose(0, 2, 1)

    selBC = np.zeros((32, 32 * 128), np.float32)
    for j in range(32):
        selBC[j, j * 128:(j + 1) * 128] = 1.0
    onesrow = np.ones((1, 128), np.float32)
    onescol_b = np.ones((128, 1), np.float32).astype(ml_dtypes.bfloat16)
    biota = np.broadcast_to(np.arange(4, dtype=np.float32)[:, None],
                            (4, 4)).copy()
    kiota = np.broadcast_to(np.arange(3, dtype=np.float32)[None, :],
                            (4, 3)).copy()
    lsirep = np.broadcast_to(lsi.astype(np.int32)[None, :], (4, 4)).copy()

    in_maps = []
    for c in range(8):
        b, h = c // 2, c % 2
        sl = slice(h * DH, (h + 1) * DH)
        bsel16 = np.zeros((4, 16), np.float32)
        bsel16[b, :] = 1.0
        in_maps.append({
            "xTd": np.ascontiguousarray(x[b].T[sl]),
            "xpwT": np.ascontiguousarray(xpwTm[sl]),
            "dtwT": np.ascontiguousarray(dtwTm[:, sl]),
            "Aarr": np.ascontiguousarray(
                A_log[sl].reshape(NDT, 128, NST).transpose(1, 0, 2)
                .reshape(128, NDT * NST)),
            "Darr": np.ascontiguousarray(Dv[sl].reshape(NDT, 128).T),
            "dtbarr": np.ascontiguousarray(dtb[sl].reshape(NDT, 128).T),
            "WTh": np.ascontiguousarray(WT[:, sl]),
            "biasFh": np.ascontiguousarray(biasF[sl].reshape(NOT, 128).T),
            "lATf": lATf,
            "lBTh": np.ascontiguousarray(
                lBT[:, :, sl].transpose(1, 0, 2).reshape(16, 3 * DH)),
            "lsirep": lsirep,
            "biota": biota,
            "kiota": kiota,
            "bsel16": bsel16,
            "selBC": selBC,
            "onesrow": onesrow,
            "onescol_b": onescol_b,
        })

    if _NC_CACHE is None:
        _NC_CACHE = build_nc()
    res = run_bass_kernel_spmd(_NC_CACHE, in_maps, core_ids=list(range(8)))

    out = np.empty((B, L, D), np.float32)
    for c in range(8):
        b, h = c // 2, c % 2
        out[b, :, h * DH:(h + 1) * DH] = res.results[c]["oq"].T
    return out
